# version 22
# baseline (speedup 1.0000x reference)
import numpy as np
import ml_dtypes
import concourse.bass as bass
import concourse.mybir as mybir
from concourse.bass_utils import run_bass_kernel_spmd

F32 = mybir.dt.float32
BF16 = mybir.dt.bfloat16
U32 = mybir.dt.uint32
AF = mybir.ActivationFunctionType
ALU = mybir.AluOpType
AX = mybir.AxisListType

NCORES = 8
T = 2048
C = 128
NCH = T // C
H = 16
DK = 64
D = 1024
INNER = 4096
LN_EPS = 1e-5
MAGIC = 12582912.0
KSUP = 4
NSUP = NCH // KSUP
ENGINES = ("sync", "tensor", "vector", "scalar", "gpsimd")


def _flat(waits):
    out = []
    for w in waits:
        if w is None:
            continue
        if isinstance(w, list):
            out.extend(x for x in w if x is not None)
        else:
            out.append(w)
    return out


class Ctx:
    def __init__(self, nc):
        self.nc = nc
        self.recs = {e: [] for e in ENGINES}
        self.sems = {}
        self.cnt = {}
        self.pending_barrier = {e: None for e in ENGINES}

    def sem(self, name):
        if name not in self.sems:
            self.sems[name] = self.nc.alloc_semaphore(name)
            self.cnt[name] = 0
        return self.sems[name]

    def op(self, eng, fn, waits=(), inc=None, inc_n=1):
        tick = None
        if inc is not None:
            self.sem(inc)
            self.cnt[inc] += inc_n
            tick = (inc, self.cnt[inc])
        ws = _flat(waits)
        if self.pending_barrier[eng]:
            ws = ws + self.pending_barrier[eng]
            self.pending_barrier[eng] = None
        for (sn, sv) in ws:
            self.sem(sn)
        self.recs[eng].append((fn, tuple(ws), tick, inc_n))
        return tick

    def barrier(self):
        snap = [(sn, self.cnt[sn]) for sn in self.sems]
        for e in ENGINES:
            prev = self.pending_barrier[e] or []
            self.pending_barrier[e] = prev + snap
        return snap

    def emit_all(self, block):
        def mk(engname):
            recs = self.recs[engname]
            sems = self.sems

            def emitter(eng):
                waited = {}
                for fn, waits, tick, inc_n in recs:
                    for (sn, sv) in waits:
                        if waited.get(sn, -1) < sv:
                            eng.wait_ge(sems[sn], sv)
                            waited[sn] = sv
                    inst = fn(eng)
                    if tick is not None:
                        inst.then_inc(sems[tick[0]], inc_n)
            return emitter

        block.sync(mk("sync"))
        block.tensor(mk("tensor"))
        block.vector(mk("vector"))
        block.scalar(mk("scalar"))
        block.gpsimd(mk("gpsimd"))


def bc(ap2d, blk):
    return ap2d.unsqueeze(2).broadcast_to([ap2d.shape[0], ap2d.shape[1], blk])


def seg(ap2d, blk):
    return ap2d.rearrange("p (a b) -> p a b", b=blk)


class DBuf:
    def __init__(self, t, w):
        self.t, self.w = t, w

    def __getitem__(self, par):
        return self.t.ap()[:, par * self.w:(par + 1) * self.w]


def build(n_chunks=NCH, debug=False, stop_after=None, skip=(), dbg_ext=False):
    nc = bass.Bass("TRN2", target_bir_lowering=False, detect_race_conditions=False)

    xd = nc.dram_tensor("x", [T, D], F32, kind="ExternalInput").ap()
    wqkvTd = nc.dram_tensor("wqkvT", [D, 3 * D], F32, kind="ExternalInput").ap()
    woTd = nc.dram_tensor("woT", [D, D], F32, kind="ExternalInput").ap()
    wgTd = nc.dram_tensor("wgT", [D, INNER], F32, kind="ExternalInput").ap()
    wuTd = nc.dram_tensor("wuT", [D, INNER], F32, kind="ExternalInput").ap()
    wdTd = nc.dram_tensor("wdT", [INNER, D], F32, kind="ExternalInput").ap()
    lgd = nc.dram_tensor("lg", [1, H], F32, kind="ExternalInput").ap()
    wsld = nc.dram_tensor("wsl", [3 * 128, INNER], F32, kind="ExternalInput").ap()
    outd = nc.dram_tensor("out", [T, D], F32, kind="ExternalOutput").ap()

    r2d = nc.dram_tensor("r2d", [T, D], F32, kind="ExternalOutput" if dbg_ext else "Internal").ap()
    sprefd = nc.dram_tensor("sprefd", [NCH * 128, 512], BF16).ap()
    wqgd = nc.dram_tensor("wqgd", [D, INNER], BF16).ap()
    wqud = nc.dram_tensor("wqud", [D, INNER], BF16).ap()
    wqdd = nc.dram_tensor("wqdd", [INNER, D], BF16).ap()
    ccin = nc.dram_tensor("ccin", [129, 512], F32).ap()
    ccout = nc.dram_tensor("ccout", [NCORES * 129, 512], F32).ap()
    ccin1 = nc.dram_tensor("ccin1", [1, 8], F32).ap()
    ccout1 = nc.dram_tensor("ccout1", [NCORES, 8], F32).ap()

    identd = nc.inline_tensor(np.eye(128, dtype=ml_dtypes.bfloat16), "identc")
    onesrd = nc.inline_tensor(np.ones((1, 128), dtype=np.float32), "onesrow")

    ctx = Ctx(nc)
    OP = ctx.op

    base, top = nc.bump_sbuf(207 * 1024)

    cur = [base]

    def At(name, shape, dtype, at=None):
        if at is None:
            at = cur[0]
            nbytes = int(np.prod(shape[1:])) * mybir.dt.size(dtype)
            cur[0] = (at + nbytes + 63) & ~63
            assert cur[0] <= base + 34 * 1024, f"persistent overflow at {name}"
        return nc.alloc_sbuf_tensor_at(name, shape, dtype, offset=at)

    ident = At("ident", [128, 128], BF16)
    onesr = At("onesr", [1, 128], F32)
    onesc = At("onesc", [128, 1], F32)
    lgrow = At("lgrow", [1, H], F32)
    lgcol = At("lgcol", [128, H], F32)
    lgstk = At("lgstk", [128, 8], F32)
    gq16 = At("gq16", [128, H], F32)
    gkv16 = At("gkv16", [128, H], F32)
    gcu = At("gcu", [128, 8], F32)
    gbl = At("gbl", [128, 8], F32)
    gbm = At("gbm", [128, 64], F32)
    tcc_off = cur[0]
    tcc = At("tcc", [128, 128], F32)
    m01 = At("m01", [128, 128], BF16)
    dmask = At("dmask", [128, H * 128], F32)
    icol = At("icol", [128, 1], F32)
    i2col = At("i2col", [128, 1], F32)
    pidu = At("pidu", [1, 1], U32)
    pidf = At("pidf", [1, 1], F32)
    jrow = At("jrow", [1, 8], F32)
    erow = At("erow", [1, 8], F32)
    birow = At("birow", [1, 8], F32)
    ebcol = At("ebcol", [128, 16], F32)
    wsrow = At("wsrow", [1, 32], F32)
    swcol = At("swcol", [128, 4], F32)
    mqcol = At("mqcol", [128, 4], F32)
    stats = At("stats", [128, 32], F32)
    kst = At("kst", [128, 96], F32)
    st2 = At("st2", [128, 32], F32)
    deq1c = At("deq1c", [128, NCH], F32)
    gscA = At("gscA", [128, NCH], F32)
    uscA = At("uscA", [128, NCH], F32)
    amx2 = At("amx2", [128, 16], F32)
    mcol = At("mcol", [128, 1], F32, at=tcc_off)
    negm = At("negm", [128, 1], F32, at=tcc_off + 64)
    wpart = At("wpart", [128, 4], F32)
    epsc = At("epsc", [128, 1], F32)
    sqscr = At("sqscr", [128, D], F32)
    s32_off = cur[0]
    S32 = At("S32", [128, 512], F32)
    SINI = At("SINI", [128, 512], F32)
    sfull_off = cur[0]
    sfull = DBuf(At("sfull", [128, 2 * 1024], BF16), 1024)
    qstg5 = At("qstg5", [128, D], F32, at=s32_off)
    qstg6 = At("qstg6", [128, D], F32, at=sfull_off)
    i1s1 = At("i1s1", [128, 512], F32, at=sfull_off)
    wqo1 = DBuf(At("wqo1", [128, 2 * 512], BF16, at=sfull_off + 2048), 512)
    sprefl = DBuf(At("sprefl", [128, 2 * 512], BF16), 512)
    xs = DBuf(At("xs", [128, 2 * D], F32), D)

    RA = base + 34 * 1024
    V_tm = At("V_tm", [128, NCH * D], BF16, at=RA)
    q_tm = At("q_tm", [128, NCH * D], BF16, at=RA + 32 * 1024)
    khat_fm = At("khat_fm", [128, 8 * T], BF16, at=RA + 64 * 1024)
    n1T = At("n1T", [128, 8 * T], BF16, at=RA + 64 * 1024)
    p_sb = At("p_sb", [128, KSUP * INNER], BF16, at=RA)
    wgq = At("wgq", [128, 8 * 1024], BF16, at=RA + 32 * 1024)
    wuq = At("wuq", [128, 8 * 1024], BF16, at=RA + 48 * 1024)

    RB = base + 130 * 1024
    h1s = DBuf(At("h1s", [128, 2 * D], BF16, at=RB), D)
    h1T = DBuf(At("h1T", [128, 2 * D], BF16, at=RB + 4 * 1024), D)
    ktmp = DBuf(At("ktmp", [128, 2 * 512], F32, at=RB + 8 * 1024), 512)
    khtm = DBuf(At("khtm", [128, 2 * D], BF16, at=RB + 12 * 1024), D)
    ktil = DBuf(At("ktil", [128, 2 * D], BF16, at=RB + 16 * 1024), D)
    qts = DBuf(At("qts", [128, 2 * D], BF16, at=RB), D)
    qpad2 = At("qpad2", [128, 2048], BF16, at=RB + 4 * 1024)
    qtpad2 = At("qtpad2", [128, 2048], BF16, at=RB + 8 * 1024)
    asb = DBuf(At("asb", [128, 2 * 256], BF16, at=RB + 12 * 1024), 256)
    i1s = At("i1s", [128, D], F32, at=RB + 13 * 1024)
    wqo = DBuf(At("wqo", [128, 2 * D], BF16, at=RB + 17 * 1024), D)
    r2c = DBuf(At("r2c", [128, 2 * D], F32, at=RB), D)
    c2s = DBuf(At("c2s", [128, 2 * D], F32, at=RB + 8 * 1024), D)
    n1tok = DBuf(At("n1tok", [128, 2 * D], BF16, at=RB + 16 * 1024), D)
    gsilu = DBuf(At("gsilu", [128, 2 * 512], F32, at=RB), 512)
    n2Ts = DBuf(At("n2Ts", [128, 2 * D], BF16, at=RB + 4 * 1024), D)
    c6s = DBuf(At("c6s", [128, 2 * D], F32, at=RB + 8 * 1024), D)
    q6s = At("q6s", [128, D], F32, at=RB + 16 * 1024)

    RC = base + 151 * 1024
    wkv = At("wkv", [128, 8 * 3 * D], BF16, at=RC)
    wstg = DBuf(At("wstg", [128, 2 * D], F32, at=RC + 48 * 1024), D)
    sg8 = At("sg8", [128, 8 * 512], F32, at=RC)
    ws8 = At("ws8", [8, 512], F32, at=RC + 16 * 1024)
    yfm = At("yfm", [128, 8 * T], BF16, at=RC)
    woTs = At("woTs", [128, 8 * D], BF16, at=RC + 32 * 1024)
    wdqg = DBuf(At("wdqg", [128, 2 * 8 * 1024], BF16, at=RC), 8 * 1024)
    racc = At("racc", [128, KSUP * D], F32, at=RC + 32 * 1024)

    assert RC + 56 * 1024 <= top, (RC, top)

    ps = [nc.alloc_psum_tensor(f"ps{i}", [128, 512], F32) for i in range(6)]
    psT = nc.alloc_psum_tensor("psT", [128, 1024], BF16)
    ps6 = nc.alloc_psum_tensor("ps6", [128, 512], F32)

    def ap(t):
        return t.ap()

    tk = {}

    def _finish_stub():
        OP("sync", lambda e: e.dma_start(out=outd[0:128, :], in_=xs[0]), inc="so", inc_n=16)
        OP("sync", lambda e: e.nop(), waits=[("so", ctx.cnt.get("so", 0)),
                                             ("sd", ctx.cnt.get("sd", 0)),
                                             ("sg", ctx.cnt.get("sg", 0))])
        with nc.Block() as block:
            ctx.emit_all(block)
        return nc


    t_id = OP("sync", lambda e: e.dma_start(out=ap(ident), in_=identd.ap()), inc="sd", inc_n=16)
    t_or = OP("sync", lambda e: e.dma_start(out=ap(onesr), in_=onesrd.ap()), inc="sd", inc_n=16)
    t_lg = OP("sync", lambda e: e.dma_start(out=ap(lgrow), in_=lgd), inc="sd", inc_n=16)
    t_pid = OP("sync", lambda e: e.dma_start(out=ap(pidu), in_=nc.partition_id_tensor.ap()),
               inc="sd", inc_n=16)
    t_wkv = None
    for k in range(8):
        t_wkv = OP("gpsimd", lambda e, k=k: e.dma_start(
            out=ap(wkv)[:, k * 3 * D:(k + 1) * 3 * D], in_=wqkvTd[k * 128:(k + 1) * 128, :]),
            inc="sg", inc_n=16)
    t1 = OP("scalar", lambda e: e.activation(ap(lgrow), ap(lgrow), AF.Sigmoid),
            waits=[t_pid], inc="ae")
    t2 = OP("scalar", lambda e: e.activation(ap(lgrow), ap(lgrow), AF.Ln), waits=[t1], inc="ae")
    t_pf = OP("vector", lambda e: e.tensor_copy(ap(pidf), ap(pidu)), waits=[t_pid], inc="ve")
    t_oc = OP("vector", lambda e: e.memset(ap(onesc), 1.0), inc="ve")
    t_ep = OP("vector", lambda e: e.memset(ap(epsc), LN_EPS), inc="ve")
    OP("vector", lambda e: e.memset(ap(kst), 0.0), inc="ve")

    t3 = OP("tensor", lambda e: e.matmul(ap(ps6)[:, 0:H], ap(onesr), ap(lgrow),
                                         start=True, stop=True), waits=[t2, t_pid], inc="pe")
    t4 = OP("vector", lambda e: e.tensor_copy(ap(lgcol), ap(ps6)[:, 0:H]), waits=[t3], inc="ve")
    t5 = None
    for hp in range(8):
        for s in range(2):
            t5 = OP("vector", lambda e, hp=hp, s=s: e.tensor_copy(
                ap(lgstk)[64 * s:64 * s + 64, hp:hp + 1],
                ap(lgcol)[64 * s:64 * s + 64, 2 * hp + s:2 * hp + s + 1]),
                waits=[t4], inc="ve")

    t6 = OP("gpsimd", lambda e: e.iota(ap(tcc), pattern=[[1, 128]], base=0,
                                       channel_multiplier=-1,
                                       allow_small_or_imprecise_dtypes=True), inc="ge")
    t7 = OP("gpsimd", lambda e: e.iota(ap(icol), pattern=[[1, 1]], base=0,
                                       channel_multiplier=1,
                                       allow_small_or_imprecise_dtypes=True), inc="ge")
    t8 = OP("gpsimd", lambda e: e.iota(ap(i2col), pattern=[[1, 1]], base=127,
                                       channel_multiplier=-1,
                                       allow_small_or_imprecise_dtypes=True), inc="ge")
    t9 = OP("gpsimd", lambda e: e.iota(ap(jrow), pattern=[[1, 8]], base=0,
                                       channel_multiplier=0,
                                       allow_small_or_imprecise_dtypes=True), inc="ge")
    t10 = OP("vector", lambda e: e.tensor_scalar(ap(m01), ap(tcc), 0.0, None, ALU.is_ge),
             waits=[t6], inc="ve")
    t11 = OP("vector", lambda e: e.tensor_scalar(ap(tcc), ap(tcc), -1.0, None, ALU.max),
             waits=[t10], inc="ve")
    t_dm = None
    for h in range(H):
        ta = OP("scalar", lambda e, h=h: e.activation(
            ap(sqscr)[:, 0:128], ap(tcc), AF.Exp, scale=ap(lgcol)[:, h:h + 1]),
            waits=[t11, t4, t_dm], inc="ae")
        t_dm = OP("vector", lambda e, h=h: e.tensor_mul(
            ap(dmask)[:, h * 128:(h + 1) * 128], ap(sqscr)[:, 0:128], ap(m01)),
            waits=[ta], inc="ve")
    t_mc = OP("vector", lambda e: e.memset(ap(mcol), MAGIC), waits=[t_dm], inc="ve")
    t_nm = OP("vector", lambda e: e.memset(ap(negm), -MAGIC), waits=[t_mc], inc="ve")
    t_gq = None
    for h in range(H):
        t_gq = OP("scalar", lambda e, h=h: e.activation(
            ap(gq16)[:, h:h + 1], ap(icol), AF.Exp,
            scale=ap(lgcol)[:, h:h + 1], bias=ap(lgcol)[:, h:h + 1]),
            waits=[t7, t4], inc="ae")
    t_gkv = None
    for h in range(H):
        t_gkv = OP("scalar", lambda e, h=h: e.activation(
            ap(gkv16)[:, h:h + 1], ap(i2col), AF.Exp, scale=ap(lgcol)[:, h:h + 1]),
            waits=[t8, t4], inc="ae")
    t_gcu = OP("scalar", lambda e: e.activation(ap(gcu), ap(lgstk), AF.Exp, scale=128.0),
               waits=[t5], inc="ae")
    t_gbl = OP("vector", lambda e: e.tensor_scalar(ap(gbl), ap(lgstk), 2048.0, None, ALU.mult),
               waits=[t5], inc="ve")
    tc1 = OP("vector", lambda e: e.tensor_scalar(ap(pidf), ap(pidf), -1.0, None, ALU.add),
             waits=[t_pf], inc="ve")
    te = OP("vector", lambda e: e.tensor_scalar(ap(erow), ap(jrow), -1.0, ap(pidf)[0:1, 0:1],
                                                ALU.mult, ALU.add), waits=[t9, tc1], inc="ve")
    tind = OP("vector", lambda e: e.tensor_single_scalar(ap(birow), ap(jrow),
                                                         ap(pidf)[0:1, 0:1], ALU.is_le),
              waits=[t9, tc1], inc="ve")
    tbi = OP("vector", lambda e: e.tensor_scalar(ap(birow), ap(birow), -1.0, 1e30,
                                                 ALU.add, ALU.mult), waits=[tind], inc="ve")
    teb = OP("tensor", lambda e: e.matmul(ap(ps6)[:, 0:8], ap(onesr), ap(erow),
                                          start=True, stop=True), waits=[te, t4], inc="pe")
    teb2 = OP("tensor", lambda e: e.matmul(ap(ps6)[:, 8:16], ap(onesr), ap(birow),
                                           start=True, stop=True), waits=[tbi], inc="pe")
    tebv = OP("vector", lambda e: e.tensor_copy(ap(ebcol), ap(ps6)[:, 0:16]),
              waits=[teb, teb2], inc="ve")
    t_gb = None
    for j in range(8):
        t_gb = OP("scalar", lambda e, j=j: e.activation(
            ap(gbm)[:, j * 8:(j + 1) * 8], ap(gbl), AF.Exp,
            scale=ap(ebcol)[:, j:j + 1], bias=ap(ebcol)[:, 8 + j:9 + j]),
            waits=[tebv, t_gbl], inc="ae")

    if stop_after == "p0":
        ctx.barrier()
        return _finish_stub()

    t_ws_last = []
    for mi in range(3):
        tb = None
        prev_dma = [None, None]
        first = True
        for c0 in range(0, INNER, 1024):
            par = (c0 // 1024) % 2
            td = OP("sync", lambda e, mi=mi, c0=c0, par=par: e.dma_start(
                out=wstg[par], in_=wsld[mi * 128:(mi + 1) * 128, c0:c0 + 1024]),
                waits=[prev_dma[par]], inc="sd", inc_n=16)
            ta = OP("scalar", lambda e, par=par: e.activation(
                ap(sqscr), wstg[par], AF.Abs, accum_out=ap(wpart)[:, 3:4]),
                waits=[td, t_dm], inc="ae")
            if first:
                tb = OP("vector", lambda e, mi=mi: e.tensor_copy(
                    ap(wpart)[:, mi:mi + 1], ap(wpart)[:, 3:4]), waits=[ta], inc="ve")
                first = False
            else:
                tb = OP("vector", lambda e, mi=mi: e.tensor_add(
                    ap(wpart)[:, mi:mi + 1], ap(wpart)[:, mi:mi + 1], ap(wpart)[:, 3:4]),
                    waits=[ta], inc="ve")
            prev_dma[par] = tb
        t_ws_last.append(tb)
    t_wsum = OP("tensor", lambda e: e.matmul(ap(ps6)[0:1, 0:3], ap(onesc),
                                             ap(wpart)[:, 0:3], start=True, stop=True),
                waits=t_ws_last + [t_oc, tebv], inc="pe")
    t_wsv = OP("vector", lambda e: e.tensor_copy(ap(stats)[0:1, 28:31], ap(ps6)[0:1, 0:3]),
               waits=[t_wsum], inc="ve")
    t_wz = OP("vector", lambda e: e.memset(ap(stats)[0:1, 8:13], 0.0), inc="ve")
    t_cw1 = OP("sync", lambda e: e.dma_start(out=ccin1[0:1, 0:3], in_=ap(stats)[0:1, 28:31]),
               waits=[t_wsv], inc="so", inc_n=16)
    t_cz1 = OP("sync", lambda e: e.dma_start(out=ccin1[0:1, 3:8], in_=ap(stats)[0:1, 8:13]),
               waits=[t_wz], inc="so", inc_n=16)
    ctx.barrier()
    t_cc1 = OP("gpsimd", lambda e: e.collective_compute(
        "AllGather", ALU.bypass, replica_groups=[list(range(NCORES))],
        ins=[ccin1.opt()], outs=[ccout1.opt()]), waits=[t_cw1, t_cz1], inc="ge")
    if stop_after == "p0w":
        return _finish_stub()

    wq_pairs = []
    for gp in (0, 2):
        for mi, (wsrc, wdst) in ((0, (wgTd, wqgd)), (1, (wuTd, wqud))):
            for r in range(0, D, 128):
                wq_pairs.append((mi, wsrc, wdst, r, gp * 1024, "cols"))
        if gp == 0:
            for kg2 in range(8):
                wq_pairs.append((2, wdTd, wqdd, kg2 * 256, 0, "rows"))
    for kg2 in range(8, 16):
        wq_pairs.append((2, wdTd, wqdd, kg2 * 256, 0, "rows"))
    wq_state = {"i": 0, "po": [None, None], "pi": [None, None], "tick": {}}
    wq_pairbufs = [wstg, xs]

    def wq_emit(npairs):
        for _ in range(npairs):
            i = wq_state["i"]
            if i >= len(wq_pairs):
                return
            wq_state["i"] += 1
            mi, wsrc, wdst, r, c0, kind = wq_pairs[i]
            eng, sem = ("scalar", "ad") if i % 2 == 0 else ("gpsimd", "sg")
            par = i % 2
            pbuf = wq_pairbufs[par]
            if kind == "cols":
                src = wsrc[r:r + 128, c0:c0 + 2048]
            else:
                src = wsrc[r:r + 256, :].rearrange("(p a) c -> p a c", p=128)
            tdl = OP(eng, lambda e, src=src, pbuf=pbuf: e.dma_start(
                out=pbuf.t.ap(), in_=src),
                waits=[wq_state["pi"][par], tk["sw"]], inc=sem, inc_n=16)
            ta = None
            for h in range(2):
                opar = h
                ta = OP("vector", lambda e, mi=mi, pbuf=pbuf, h=h: e.tensor_scalar(
                    ap(i1s), pbuf[h], ap(swcol)[:, mi:mi + 1], -1.0, ALU.mult, ALU.max),
                    waits=[tdl], inc="ve")
                tb = OP("vector", lambda e: e.tensor_scalar(
                    ap(i1s), ap(i1s), 1.0, MAGIC, ALU.min, ALU.add),
                    waits=[ta], inc="ve")
                tiq = OP("vector", lambda e, opar=opar: e.tensor_scalar(
                    wqo[opar], ap(i1s), MAGIC, None, ALU.subtract),
                    waits=[tb, wq_state["po"][opar]], inc="ve")
                if kind == "cols":
                    dst = wdst[r:r + 128, c0 + h * 1024:c0 + (h + 1) * 1024]
                    key = (mi, r, c0 + h * 1024)
                else:
                    dst = wdst[r:r + 256, :].rearrange("(p a) c -> p a c", p=128)[:, h, :]
                    key = (mi, r + h, c0)
                tdo = OP("sync", lambda e, dst=dst, opar=opar: e.dma_start(
                    out=dst, in_=wqo[opar]), waits=[tiq], inc="so", inc_n=16)
                wq_state["po"][opar] = tdo
                wq_state["tick"][key] = tdo
            wq_state["pi"][par] = ta
            if kind == "rows":
                wq_state["tick"][(2, r, 0)] = tdo
                wq_state["tick"][(2, r + 128, 0)] = tdo

    tk["xfree"] = [None] * n_chunks
    tk["qkv_ev"] = [[None] * 6 for _ in range(n_chunks)]
    tk["mmn"] = [[None] * 6 for _ in range(n_chunks)]
    tk["vready"] = [None] * n_chunks
    tk["q_ev"] = [None] * n_chunks
    tk["khfm"] = [None] * n_chunks
    tk["spref"] = [None] * n_chunks
    tk["schain"] = [None] * n_chunks
    tk["psT_free"] = [None] * n_chunks
    tk["ksq_rd"] = [None] * n_chunks
    tk["h1tr"] = [None] * n_chunks
    tk["delta"] = [None] * n_chunks
    tk["ktilg"] = [None] * n_chunks
    tk["sprefdma"] = [None] * n_chunks
    STATS = ap(stats)
    KST = ap(kst)

    t_s0 = OP("vector", lambda e: e.memset(ap(S32), 0.0), inc="ve")
    tk["schain_prev"] = t_s0
    defer1 = {}

    def emit_p1tail(cc2, tE_next, extra_psT=None):
        st = defer1[cc2]
        par2 = st["par"]
        tkh0, tkh1 = st["tkh0"], st["tkh1"]
        tkt = OP("vector", lambda e, par2=par2: e.tensor_tensor(
            seg(ktil[par2], 64), seg(khtm[par2], 64), bc(ap(gkv16), 64), ALU.mult),
            waits=[tkh0, tkh1, t_gkv, tk["delta"][cc2 - 2] if cc2 >= 2 else None],
            inc="ve")
        tk["ktilg"][cc2] = tkt
        tT2 = None
        for j in range(8):
            tT2 = OP("tensor", lambda e, j=j, par2=par2: e.transpose(
                ap(psT)[:, j * 128:(j + 1) * 128], khtm[par2][:, j * 128:(j + 1) * 128],
                ap(ident)), waits=[tkh0, tkh1, tE_next, extra_psT] if j == 0 else [],
                inc="pe" if j == 7 else None)
        tE2 = None
        for j in range(8):
            tE2 = OP("vector", lambda e, j=j, cc2=cc2: e.tensor_copy(
                ap(khat_fm)[:, j * T + cc2 * 128: j * T + (cc2 + 1) * 128],
                ap(psT)[:, j * 128:(j + 1) * 128]),
                waits=[tT2] if j == 0 else [], inc="ve" if j == 7 else None)
        tk["khfm"][cc2] = tE2
        tk["psT_free"][cc2] = tE2
        wprev = tk["schain"][cc2 - 1] if cc2 >= 1 else tk["schain_prev"]
        tsp = OP("vector", lambda e, par2=par2: e.tensor_copy(sprefl[par2], ap(S32)),
                 waits=[wprev, tk["sprefdma"][cc2 - 2] if cc2 >= 2 else None], inc="ve")
        tspd = OP("sync", lambda e, cc2=cc2, par2=par2: e.dma_start(
            out=sprefd[cc2 * 128:(cc2 + 1) * 128, :], in_=sprefl[par2]),
            waits=[tsp], inc="so", inc_n=16)
        tk["spref"][cc2] = tspd
        tk["sprefdma"][cc2] = tspd
        tdm = None
        for h in range(H):
            hp, s = h // 2, h % 2
            tdm = OP("tensor", lambda e, h=h, hp=hp, s=s, par2=par2, cc2=cc2: e.matmul(
                ap(ps[3])[64 * s:64 * s + 64, hp * 64:(hp + 1) * 64],
                ktil[par2][:, h * 64:(h + 1) * 64],
                ap(V_tm)[:, cc2 * D + h * 64: cc2 * D + (h + 1) * 64],
                start=True, stop=True, tile_position=(0, 64 * s)),
                waits=[tkt, tk["vready"][cc2], wprev] if h == 0 else [],
                inc="pe" if h == H - 1 else None)
        tk["delta"][cc2] = tdm
        tu1 = OP("vector", lambda e: e.tensor_tensor(seg(ap(S32), 64), seg(ap(S32), 64),
                                                     bc(ap(gcu), 64), ALU.mult),
                 waits=[tsp, t_gcu], inc="ve")
        tu2 = OP("vector", lambda e: e.tensor_add(ap(S32), ap(S32), ap(ps[3])),
                 waits=[tu1, tdm], inc="ve")
        tk["schain"][cc2] = tu2

    for cc in range(n_chunks):
        if cc == 8:
            t_rb1 = OP("sync", lambda e: e.dma_start(
                out=ap(stats)[0:8, 8:16], in_=ccout1), waits=[t_cc1], inc="sd", inc_n=16)
            t_m1 = OP("tensor", lambda e: e.matmul(
                ap(ps6)[0:1, 8:16], ap(onesc)[0:8, 0:1], ap(stats)[0:8, 8:16],
                start=True, stop=True), waits=[t_rb1, t_oc], inc="pe")
            t_w2 = OP("vector", lambda e: e.tensor_scalar(
                ap(wsrow)[0:1, 0:3], ap(ps6)[0:1, 8:11], 1.0 / (D * INNER), 1e-5,
                ALU.mult, ALU.max), waits=[t_m1], inc="ve")
            t_w3 = OP("tensor", lambda e: e.matmul(
                ap(ps6)[:, 0:3], ap(onesr), ap(wsrow)[0:1, 0:3],
                start=True, stop=True), waits=[t_w2], inc="pe")
            t_w4 = OP("vector", lambda e: e.reciprocal(ap(swcol)[:, 0:3], ap(ps6)[:, 0:3]),
                      waits=[t_w3], inc="ve")
            t_w5 = OP("vector", lambda e: e.tensor_copy(ap(mqcol)[:, 0:3], ap(ps6)[:, 0:3]),
                      waits=[t_w3], inc="ve")
            t_w6 = OP("vector", lambda e: e.tensor_scalar(
                ap(mqcol)[:, 3:4], ap(mqcol)[:, 2:3], 1.0 / 127.0, None, ALU.mult),
                waits=[t_w5], inc="ve")
            tk["sw"] = [t_w4, t_w5, t_w6]
        par = cc % 2
        xpap = xs[par]
        w_free = tk["xfree"][cc - 2] if cc >= 2 else None
        txd = OP("sync", lambda e, cc=cc, xpap=xpap: e.dma_start(
            out=xpap, in_=xd[cc * 128:(cc + 1) * 128, :]), waits=[w_free], inc="sd", inc_n=16)
        col = 16 * par
        ts1 = OP("vector", lambda e, xpap=xpap, col=col: e.reduce_sum(
            STATS[:, col:col + 1], xpap, axis=AX.X), waits=[txd], inc="ve")
        ts2 = OP("scalar", lambda e, xpap=xpap, col=col: e.activation(
            ap(sqscr), xpap, AF.Square, accum_out=STATS[:, col + 1:col + 2]),
            waits=[txd, tk["ksq_rd"][cc - 1] if cc >= 1 else None], inc="ae")
        tmu = OP("vector", lambda e, col=col: e.tensor_scalar(
            STATS[:, col + 2:col + 3], STATS[:, col:col + 1], 1.0 / D, None, ALU.mult),
            waits=[ts1], inc="ve")
        tms = OP("vector", lambda e, col=col: e.tensor_mul(
            STATS[:, col + 3:col + 4], STATS[:, col + 2:col + 3], STATS[:, col + 2:col + 3]),
            waits=[tmu], inc="ve")
        tvr = OP("vector", lambda e, col=col: e.scalar_tensor_tensor(
            STATS[:, col + 4:col + 5], STATS[:, col + 1:col + 2], 1.0 / D,
            STATS[:, col + 3:col + 4], ALU.mult, ALU.subtract), waits=[tms, ts2], inc="ve")
        tsd = OP("scalar", lambda e, col=col: e.activation(
            STATS[:, col + 5:col + 6], STATS[:, col + 4:col + 5], AF.Sqrt, bias=ap(epsc)),
            waits=[tvr, t_ep], inc="ae")
        trs = OP("vector", lambda e, col=col: e.reciprocal(
            STATS[:, col + 6:col + 7], STATS[:, col + 5:col + 6]), waits=[tsd], inc="ve")
        tnm = OP("vector", lambda e, col=col: e.scalar_tensor_tensor(
            STATS[:, col + 7:col + 8], STATS[:, col + 2:col + 3], -1.0,
            STATS[:, col + 6:col + 7], ALU.mult, ALU.mult), waits=[trs], inc="ve")
        h1ap = h1s[par]
        th1 = OP("scalar", lambda e, h1ap=h1ap, xpap=xpap, col=col: e.activation(
            h1ap, xpap, AF.Identity, scale=STATS[:, col + 6:col + 7],
            bias=STATS[:, col + 7:col + 8]),
            waits=[tnm, tk["h1tr"][cc - 2] if cc >= 2 else None], inc="ae")
        tk["xfree"][cc] = [th1, ts1]
        if cc == 0:
            wpsT = None
        elif cc == 1:
            wpsT = defer1[0]["tE"]
        else:
            wpsT = tk["psT_free"][cc - 2]
        tT = None
        for j in range(8):
            tT = OP("tensor", lambda e, j=j, h1ap=h1ap: e.transpose(
                ap(psT)[:, j * 128:(j + 1) * 128], h1ap[:, j * 128:(j + 1) * 128], ap(ident)),
                waits=[th1, wpsT, t_id] if j == 0 else [], inc="pe" if j == 7 else None)
        tk["h1tr"][cc] = tT
        h1Tap = h1T[par]
        tE = OP("vector", lambda e, h1Tap=h1Tap: e.tensor_copy(h1Tap, ap(psT)),
                waits=[tT], inc="ve")

        kcol = 48 * par
        prev_tr2 = [None]
        tkh = [None, None]
        if "qkv" in skip:
            for n in range(6):
                tk["qkv_ev"][cc][n] = tE
            tk["q_ev"][cc] = tE
            tk["vready"][cc] = tE
            tk["schain"][cc] = tk["schain"][cc - 1] if cc >= 1 else tk["schain_prev"]
            tk["spref"][cc] = None
            tk["sprefdma"][cc] = None
            tk["delta"][cc] = None
            tk["ktilg"][cc] = None
            tk["khfm"][cc] = tE
            tk["psT_free"][cc] = tE
            tk["ksq_rd"][cc] = None
            continue
        for n in range(6):
            bank = ps[n % 3]
            if n >= 3:
                bfree = tk["qkv_ev"][cc][n - 3]
            elif cc > 0:
                bfree = tk["qkv_ev"][cc - 1][n + 3]
            else:
                bfree = None
            tm = None
            for k in range(8):
                tm = OP("tensor", lambda e, n=n, k=k, bank=bank, h1Tap=h1Tap: e.matmul(
                    ap(bank), h1Tap[:, k * 128:(k + 1) * 128],
                    ap(wkv)[:, k * 3 * D + n * 512: k * 3 * D + (n + 1) * 512],
                    start=(k == 0), stop=(k == 7)),
                    waits=[tE, bfree, t_wkv] if k == 0 else [], inc="pe" if k == 7 else None)
            tk["mmn"][cc][n] = tm
            if n < 2:
                tk["qkv_ev"][cc][n] = OP("scalar", lambda e, n=n, bank=bank, cc=cc: e.activation(
                    ap(q_tm)[:, cc * D + n * 512: cc * D + (n + 1) * 512], ap(bank), AF.Copy),
                    waits=[tm], inc="ae")
            elif n >= 4:
                tk["qkv_ev"][cc][n] = OP("scalar", lambda e, n=n, bank=bank, cc=cc: e.activation(
                    ap(V_tm)[:, cc * D + (n - 4) * 512: cc * D + (n - 3) * 512], ap(bank),
                    AF.Copy), waits=[tm], inc="ae")
            elif "kln" in skip:
                tk["qkv_ev"][cc][n] = OP("scalar", lambda e, n=n, bank=bank, par=par: e.activation(
                    khtm[par][:, (n - 2) * 512:(n - 1) * 512], ap(bank), AF.Copy),
                    waits=[tm], inc="ae")
                tkh[n - 2] = tk["qkv_ev"][cc][n]
            else:
                off = (n - 2) * 8
                tr1 = OP("vector", lambda e, bank=bank, off=off, kcol=kcol: e.reduce_sum(
                    KST[:, kcol + off: kcol + off + 8], seg(ap(bank), 64), axis=AX.X),
                    waits=[tm], inc="ve")
                tmu2 = OP("vector", lambda e, off=off, kcol=kcol: e.tensor_scalar(
                    KST[:, kcol + off: kcol + off + 8], KST[:, kcol + off: kcol + off + 8],
                    1.0 / DK, None, ALU.mult), waits=[tr1], inc="ve")
                tc2 = OP("vector", lambda e, bank=bank, off=off, par=par, kcol=kcol: e.tensor_tensor(
                    seg(ktmp[par], 64), seg(ap(bank), 64),
                    bc(KST[:, kcol + off: kcol + off + 8], 64), ALU.subtract),
                    waits=[tmu2], inc="ve")
                tq = OP("scalar", lambda e, par=par: e.activation(
                    ap(sqscr)[:, 0:512], ktmp[par], AF.Square),
                    waits=[tc2, prev_tr2[0]], inc="ae")
                tr2 = OP("vector", lambda e, off=off, kcol=kcol: e.reduce_sum(
                    KST[:, kcol + 16 + off: kcol + 16 + off + 8], seg(ap(sqscr)[:, 0:512], 64),
                    axis=AX.X), waits=[tq], inc="ve")
                prev_tr2[0] = tr2
                tvr2 = OP("vector", lambda e, off=off, kcol=kcol: e.tensor_scalar(
                    KST[:, kcol + 16 + off: kcol + 16 + off + 8],
                    KST[:, kcol + 16 + off: kcol + 16 + off + 8], 1.0 / DK, None, ALU.mult),
                    waits=[tr2], inc="ve")
                tsd2 = OP("scalar", lambda e, off=off, kcol=kcol: e.activation(
                    KST[:, kcol + 16 + off: kcol + 16 + off + 8],
                    KST[:, kcol + 16 + off: kcol + 16 + off + 8], AF.Sqrt, bias=ap(epsc)),
                    waits=[tvr2, t_ep], inc="ae")
                trc = OP("vector", lambda e, off=off, kcol=kcol: e.reciprocal(
                    KST[:, kcol + 16 + off: kcol + 16 + off + 8],
                    KST[:, kcol + 16 + off: kcol + 16 + off + 8]), waits=[tsd2], inc="ve")
                tkh[n - 2] = OP("vector", lambda e, off=off, n=n, par=par, kcol=kcol: e.tensor_tensor(
                    seg(khtm[par][:, (n - 2) * 512:(n - 1) * 512], 64), seg(ktmp[par], 64),
                    bc(KST[:, kcol + 16 + off: kcol + 16 + off + 8], 64), ALU.mult),
                    waits=[trc, tk["ktilg"][cc - 2] if cc >= 2 else None], inc="ve")
                tk["qkv_ev"][cc][n] = tkh[n - 2]
        tk["q_ev"][cc] = tk["qkv_ev"][cc][1]
        tk["vready"][cc] = tk["qkv_ev"][cc][5]
        defer1[cc] = {"par": par, "tkh0": tkh[0], "tkh1": tkh[1], "tE": tE}
        if cc >= 1:
            emit_p1tail(cc - 1, tE)
        if cc == n_chunks - 1:
            emit_p1tail(cc, tE, extra_psT=tk["psT_free"][cc - 1])
        tk["ksq_rd"][cc] = prev_tr2[0]

    ctx.barrier()
    if stop_after == "p1":
        return _finish_stub()

    if debug:
        dbg_k = nc.dram_tensor("dbg_k", [128, 8 * T], BF16, kind="ExternalOutput" if dbg_ext else "Internal").ap()
        dbg_v = nc.dram_tensor("dbg_v", [128, NCH * D], BF16, kind="ExternalOutput" if dbg_ext else "Internal").ap()
        dbg_q = nc.dram_tensor("dbg_q", [128, NCH * D], BF16, kind="ExternalOutput" if dbg_ext else "Internal").ap()
        dbg_t = nc.dram_tensor("dbg_t", [128, 256], F32, kind="ExternalOutput" if dbg_ext else "Internal").ap()
        OP("sync", lambda e: e.dma_start(out=dbg_k, in_=ap(khat_fm)), inc="so", inc_n=16)
        OP("sync", lambda e: e.dma_start(out=dbg_v, in_=ap(V_tm)), inc="so", inc_n=16)
        OP("sync", lambda e: e.dma_start(out=dbg_q, in_=ap(q_tm)), inc="so", inc_n=16)
        OP("sync", lambda e: e.dma_start(out=dbg_t[:, 0:16], in_=ap(gkv16)), inc="so", inc_n=16)
        OP("sync", lambda e: e.dma_start(out=dbg_t[:, 16:32], in_=ap(gq16)), inc="so", inc_n=16)
        OP("sync", lambda e: e.dma_start(out=dbg_t[:, 32:40], in_=ap(gcu)), inc="so", inc_n=16)
        OP("sync", lambda e: e.dma_start(out=dbg_t[:, 40:56], in_=ap(lgcol)), inc="so", inc_n=16)
        OP("sync", lambda e: e.dma_start(out=dbg_t[:, 56:152], in_=ap(kst)), inc="so", inc_n=16)
        OP("sync", lambda e: e.dma_start(out=dbg_t[:, 152:216], in_=ap(gbm)), inc="so", inc_n=16)
        ctx.barrier()

    t_ccs = OP("sync", lambda e: e.dma_start(out=ccin[0:128, :], in_=ap(S32)),
               waits=[tk["schain"][n_chunks - 1]], inc="so", inc_n=16)
    t_ccw = OP("sync", lambda e: e.dma_start(out=ccin[128:129, 0:3], in_=ap(stats)[0:1, 28:31]),
               waits=[t_wsv], inc="so", inc_n=16)
    t_ccq = OP("vector", lambda e: e.memset(ap(sqscr)[0:1, 0:512], 0.0), inc="ve")
    t_ccz = OP("sync", lambda e: e.dma_start(out=ccin[128:129, 3:512], in_=ap(sqscr)[0:1, 0:509]),
               waits=[t_ccq], inc="so", inc_n=16)
    t_cc = OP("gpsimd", lambda e: e.collective_compute(
        "AllGather", ALU.bypass, replica_groups=[list(range(NCORES))],
        ins=[ccin.opt()], outs=[ccout.opt()]), waits=[t_ccs, t_ccw, t_ccz], inc="ge")
    t_wo = None
    for k in range(8):
        t_wo = OP("gpsimd", lambda e, k=k: e.dma_start(
            out=ap(woTs)[:, k * D:(k + 1) * D], in_=woTd[k * 128:(k + 1) * 128, :]),
            waits=[t_cc] if k == 0 else [], inc="sg", inc_n=16)
    t_sg = [None] * 8
    for j in range(8):
        t_sg[j] = OP("sync", lambda e, j=j: e.dma_start(
            out=ap(sg8)[:, j * 512:(j + 1) * 512], in_=ccout[129 * j:129 * j + 128, :]),
            waits=[t_cc], inc="sd", inc_n=16)
    t_si = OP("vector", lambda e: e.memset(ap(SINI), 0.0), inc="ve")
    for j in range(8):
        tmj = OP("vector", lambda e, j=j: e.tensor_tensor(
            seg(ap(sg8)[:, j * 512:(j + 1) * 512], 64), seg(ap(sg8)[:, j * 512:(j + 1) * 512], 64),
            bc(ap(gbm)[:, j * 8:(j + 1) * 8], 64), ALU.mult), waits=[t_sg[7], t_gb], inc="ve")
        t_si = OP("vector", lambda e, j=j: e.tensor_add(
            ap(SINI), ap(SINI), ap(sg8)[:, j * 512:(j + 1) * 512]), waits=[tmj, t_si], inc="ve")

    if debug:
        dbg_sg = nc.dram_tensor("dbg_sg", [128, 8 * 512 + 512], F32,
                                kind="ExternalOutput" if dbg_ext else "Internal").ap()
        dbg_cc = nc.dram_tensor("dbg_cc", [NCORES * 129, 512], F32,
                                kind="ExternalOutput" if dbg_ext else "Internal").ap()
        OP("sync", lambda e: e.dma_start(out=dbg_sg[:, 0:4096], in_=ap(sg8)),
           waits=[t_si], inc="so", inc_n=16)
        OP("sync", lambda e: e.dma_start(out=dbg_sg[:, 4096:4608], in_=ap(SINI)),
           waits=[t_si], inc="so", inc_n=16)
        OP("sync", lambda e: e.dma_start(out=dbg_cc, in_=ccout),
           waits=[t_cc], inc="so", inc_n=16)

    ctx.barrier()
    if stop_after == "p2":
        return _finish_stub()

    tk["yev"] = [[None] * 8 for _ in range(n_chunks)]
    tk["mkhist"] = [None] * n_chunks
    tk["sfull"] = [None] * n_chunks
    tk["uchain"] = [None] * n_chunks
    tk["p3T_free"] = [None] * n_chunks
    tk["p3mm"] = [None] * n_chunks
    tk["qts_rd"] = [None] * n_chunks
    tk["uchain_prev"] = t_si

    t_z1 = OP("vector", lambda e: e.memset(ap(qpad2), 0.0), inc="ve")
    t_z2 = OP("vector", lambda e: e.memset(ap(qtpad2), 0.0), inc="ve")
    t_z3 = OP("vector", lambda e: e.memset(sfull[0], 0.0), inc="ve")
    t_z4 = OP("vector", lambda e: e.memset(sfull[1], 0.0), inc="ve")
    tk["p3zero"] = [t_z1, t_z2, t_z3, t_z4]

    def bd3(apx, rows, blk):
        return apx[rows[0]:rows[1], :].rearrange("p (a b) -> p a b", b=128)[:, :, 0:blk] \
            if blk != 128 else apx[rows[0]:rows[1], :].rearrange("p (a b) -> p a b", b=128)

    SINIg = S32
    gcu2 = [gcu, gbl]
    texp_hist = [None, None]
    tuc_hist = [tk["schain"][n_chunks - 1], t_gb]
    def emit_sfull(cc, par, tld, pmw):
        texp = OP("scalar", lambda e, cc=cc, par=par: e.activation(
            ap(gcu2[par]), ap(lgstk), AF.Exp, scale=128.0 * cc),
            waits=[t5, tuc_hist[par]], inc="ae")
        texp_hist[par] = texp
        tuc = OP("vector", lambda e, par=par: e.tensor_tensor(
            seg(ap(SINIg), 64), seg(ap(SINI), 64), bc(ap(gcu2[par]), 64), ALU.mult),
            waits=[t_si, texp, t_ccs], inc="ve")
        tuc_hist[par] = tuc
        tk["uchain"][cc] = tuc
        tsfa = OP("vector", lambda e, par=par: e.scalar_tensor_tensor(
            bd3(sfull[par], (0, 64), 64), seg(ap(SINIg)[0:64, :], 64), 1.0,
            seg(sprefl[par][0:64, :], 64), ALU.mult, ALU.add),
            waits=[tuc, tld, tk["p3zero"], pmw], inc="ve")
        tsf = OP("vector", lambda e, par=par: e.scalar_tensor_tensor(
            sfull[par][64:128, :].rearrange("p (a b) -> p a b", b=128)[:, :, 64:128],
            seg(ap(SINIg)[64:128, :], 64), 1.0,
            seg(sprefl[par][64:128, :], 64), ALU.mult, ALU.add),
            waits=[tsfa], inc="ve")
        tk["sfull"][cc] = tsf
        return tsf

    defer_st = {}
    for cc in range(n_chunks):
        par = cc % 2
        wfree = tk["sfull"][cc - 2] if cc >= 2 else None
        tld = OP("sync", lambda e, cc=cc, par=par: e.dma_start(
            out=sprefl[par], in_=sprefd[cc * 128:(cc + 1) * 128, :]),
            waits=[tk["spref"][cc], wfree], inc="sd", inc_n=16)
        if cc >= 2:
            tsf = emit_sfull(cc, par, tld, tk["p3mm"][cc - 2])
        else:
            defer_st[cc] = tld
            tsf = None
        tqt = OP("vector", lambda e, par=par, cc=cc: e.tensor_tensor(
            seg(qts[par], 64), seg(ap(q_tm)[:, cc * D:(cc + 1) * D], 64),
            bc(ap(gq16), 64), ALU.mult),
            waits=[tk["q_ev"][cc], t_gq, tk["qts_rd"][cc - 2] if cc >= 2 else None], inc="ve")
        wT = tk["p3T_free"][cc - 1] if cc >= 1 else None
        tTq = None
        for j in range(8):
            tTq = OP("tensor", lambda e, j=j, cc=cc: e.transpose(
                ap(psT)[:, j * 128:(j + 1) * 128],
                ap(q_tm)[:, cc * D + j * 128: cc * D + (j + 1) * 128], ap(ident)),
                waits=[wT, tk["q_ev"][cc]] if j == 0 else [], inc="pe" if j == 7 else None)
        tEq = None
        for a in range(8):
            tEq = OP("scalar", lambda e, a=a: e.activation(
                ap(qpad2)[0:64, a * 256: a * 256 + 128],
                ap(psT)[0:64, a * 128:(a + 1) * 128], AF.Copy),
                waits=[tTq, tk["p3zero"]] if a == 0 else [], inc="ae" if a == 7 else None)
        for a in range(8):
            tEq = OP("scalar", lambda e, a=a: e.activation(
                ap(qpad2)[64:128, a * 256 + 128: a * 256 + 256],
                ap(psT)[64:128, a * 128:(a + 1) * 128], AF.Copy),
                waits=[], inc="ae" if a == 7 else None)
        def emit_qtpad(par2, wT2, tqt2):
            tTq2 = None
            for j in range(8):
                tTq2 = OP("tensor", lambda e, j=j, par2=par2: e.transpose(
                    ap(psT)[:, j * 128:(j + 1) * 128], qts[par2][:, j * 128:(j + 1) * 128],
                    ap(ident)), waits=[tqt2, wT2] if j == 0 else [],
                    inc="pe" if j == 7 else None)
            tEq2 = None
            for a in range(8):
                tEq2 = OP("scalar", lambda e, a=a: e.activation(
                    ap(qtpad2)[0:64, a * 256: a * 256 + 128],
                    ap(psT)[0:64, a * 128:(a + 1) * 128], AF.Copy),
                    waits=[tTq2] if a == 0 else [], inc="ae" if a == 7 else None)
            for a in range(8):
                tEq2 = OP("scalar", lambda e, a=a: e.activation(
                    ap(qtpad2)[64:128, a * 256 + 128: a * 256 + 256],
                    ap(psT)[64:128, a * 128:(a + 1) * 128], AF.Copy),
                    waits=[], inc="ae" if a == 7 else None)
            return tTq2, tEq2

        if cc >= 2:
            tTq2, tEq2 = emit_qtpad(par, tEq, tqt)
            tk["qts_rd"][cc] = tTq2
            tk["p3T_free"][cc] = tEq2
        else:
            defer_st[(cc, "tqt")] = tqt
            defer_st[(cc, "tEq")] = tEq
            tEq2 = None
            tk["p3T_free"][cc] = tEq

        tmk_l = [None] * 8
        tsc_l = {}

        def emit_tsc(hp):
            sbank = ps[hp % 2]
            if hp >= 2:
                mkfree = tmk_l[hp - 2]
            elif cc >= 1:
                mkfree = tk["mkhist"][cc - 1][hp + 6]
            else:
                mkfree = None
            tsc_l[hp] = OP("tensor", lambda e, hp=hp, sbank=sbank, cc=cc: e.matmul(
                ap(sbank)[:, 0:256],
                ap(khat_fm)[:, hp * T + cc * 128: hp * T + (cc + 1) * 128],
                ap(qpad2)[:, hp * 256:(hp + 1) * 256],
                start=True, stop=True),
                waits=[tEq, tk["khfm"][cc], mkfree], inc="pe")

        emit_tsc(0)
        for hp in range(8):
            if hp + 1 < 8:
                emit_tsc(hp + 1)
            ybank = ps[2 + hp % 2]
            apar = hp % 2
            if hp >= 2:
                sfree = tk["yev"][cc][hp - 2]
            elif cc >= 1:
                sfree = tk["yev"][cc - 1][6 + hp]
            else:
                sfree = None
            tmk = OP("vector", lambda e, hp=hp, apar=apar: e.tensor_mul(
                asb[apar], ap(ps[hp % 2])[:, 0:256], ap(dmask)[:, hp * 256:(hp + 1) * 256]),
                waits=[tsc_l[hp], t_dm, sfree], inc="ve")
            tmk_l[hp] = tmk
            if cc >= 2:
                tym = OP("tensor", lambda e, hp=hp, ybank=ybank, cc=cc, apar=apar: e.matmul(
                    ap(ybank)[:, 0:256],
                    ap(V_tm)[:, cc * D + hp * 128: cc * D + (hp + 1) * 128],
                    asb[apar], start=True, stop=False),
                    waits=[tmk, tsf, tEq2, tk["vready"][cc], sfree], inc=None)
                tym = OP("tensor", lambda e, hp=hp, ybank=ybank, par=par: e.matmul(
                    ap(ybank)[:, 0:256], sfull[par][:, hp * 128:(hp + 1) * 128],
                    ap(qtpad2)[:, hp * 256:(hp + 1) * 256],
                    start=False, stop=True), waits=[], inc="pe")
                tk["p3mm"][cc] = tym
            else:
                tym = OP("tensor", lambda e, hp=hp, ybank=ybank, cc=cc, apar=apar: e.matmul(
                    ap(ybank)[:, 0:256],
                    ap(V_tm)[:, cc * D + hp * 128: cc * D + (hp + 1) * 128],
                    asb[apar], start=True, stop=True),
                    waits=[tmk, tk["vready"][cc], sfree], inc="pe")
            tya = OP("scalar", lambda e, hp=hp, ybank=ybank, cc=cc: e.activation(
                ap(yfm)[0:64, hp * T + cc * 128: hp * T + (cc + 1) * 128],
                ap(ybank)[0:64, 0:128], AF.Copy), waits=[tym], inc="ae")
            tk["yev"][cc][hp] = OP("scalar", lambda e, hp=hp, ybank=ybank, cc=cc: e.activation(
                ap(yfm)[64:128, hp * T + cc * 128: hp * T + (cc + 1) * 128],
                ap(ybank)[64:128, 128:256], AF.Copy), waits=[tya], inc="ae")
            if hp in (2, 5):
                wq_emit(1)
        tk["mkhist"][cc] = tmk_l
        wq_emit(1)

        if cc == 1:
            tad_hist = {}
            for dcc in (0, 1):
                dpar = dcc % 2
                tsf_d = emit_sfull(dcc, dpar, defer_st[dcc], None)
                tTq2, tEq2d = emit_qtpad(dpar, tk["p3T_free"][1],
                                         defer_st[(dcc, "tqt")])
                tk["qts_rd"][dcc] = tTq2
                tk["p3T_free"][dcc] = tEq2d
                tcx_l = [None] * 8
                for hp in range(8):
                    cbank = ps[4 + hp % 2]
                    tcx = OP("tensor", lambda e, hp=hp, cbank=cbank, dpar=dpar: e.matmul(
                        ap(cbank)[:, 0:256], sfull[dpar][:, hp * 128:(hp + 1) * 128],
                        ap(qtpad2)[:, hp * 256:(hp + 1) * 256],
                        start=True, stop=True),
                        waits=[tsf_d, tEq2d, tad_hist.get(hp % 2)], inc="pe")
                    tcx_l[hp] = tcx
                    ta1 = OP("vector", lambda e, hp=hp, cbank=cbank, dcc=dcc: e.scalar_tensor_tensor(
                        ap(yfm)[0:64, hp * T + dcc * 128: hp * T + (dcc + 1) * 128],
                        ap(cbank)[0:64, 0:128], 1.0,
                        ap(yfm)[0:64, hp * T + dcc * 128: hp * T + (dcc + 1) * 128],
                        ALU.mult, ALU.add),
                        waits=[tcx, tk["yev"][dcc][hp]], inc="ve")
                    ta2 = OP("vector", lambda e, hp=hp, cbank=cbank, dcc=dcc: e.scalar_tensor_tensor(
                        ap(yfm)[64:128, hp * T + dcc * 128: hp * T + (dcc + 1) * 128],
                        ap(cbank)[64:128, 128:256], 1.0,
                        ap(yfm)[64:128, hp * T + dcc * 128: hp * T + (dcc + 1) * 128],
                        ALU.mult, ALU.add),
                        waits=[ta1], inc="ve")
                    tad_hist[hp % 2] = ta2
                    tk["yev"][dcc][hp] = ta2
                tk["p3mm"][dcc] = tcx_l[7]
    wq_emit(len(wq_pairs))

    ctx.barrier()
    if stop_after == "p3":
        return _finish_stub()

    if debug:
        dbg_y = nc.dram_tensor("dbg_y", [128, 8 * T], BF16, kind="ExternalOutput" if dbg_ext else "Internal").ap()
        OP("sync", lambda e: e.dma_start(out=dbg_y, in_=ap(yfm)), inc="so", inc_n=16)
        ctx.barrier()

    tk["r2free"] = [None] * n_chunks
    tk["trn45"] = [None] * n_chunks

    def emit_n1T(cc2):
        trn2, par2 = tk["trn45"][cc2]
        wT = tk["p45T_free"][cc2 - 1] if cc2 >= 1 else None
        tT = None
        for j in range(8):
            tT = OP("tensor", lambda e, j=j, par2=par2: e.transpose(
                ap(psT)[:, j * 128:(j + 1) * 128], n1tok[par2][:, j * 128:(j + 1) * 128],
                ap(ident)), waits=[trn2, wT] if j == 0 else [], inc="pe" if j == 7 else None)
        tk["n1tr"][cc2] = tT
        tEv = None
        for j in range(8):
            tEv = OP("vector", lambda e, j=j, cc2=cc2: e.tensor_scalar(
                ap(n1T)[:, j * T + cc2 * 128: j * T + (cc2 + 1) * 128],
                ap(psT)[:, j * 128:(j + 1) * 128], -128.0, 127.0, ALU.max, ALU.min),
                waits=[tT] if j == 0 else [], inc="ve" if j == 7 else None)
        tk["n1T"][cc2] = tEv
        tk["p45T_free"][cc2] = tEv
    tk["n1T"] = [None] * n_chunks
    tk["p45T_free"] = [None] * n_chunks
    tk["r2cfree"] = [None] * n_chunks
    tk["c2free"] = [None] * n_chunks
    tk["n1tr"] = [None] * n_chunks
    tk["x2"] = [None] * n_chunks
    ST2 = ap(st2)

    for cc in range(n_chunks):
        par = cc % 2
        xpap = xs[par]
        xfree = tk["r2free"][cc - 2] if cc >= 2 else None
        tx2 = OP("sync", lambda e, cc=cc, xpap=xpap: e.dma_start(
            out=xpap, in_=xd[cc * 128:(cc + 1) * 128, :]), waits=[xfree], inc="sd", inc_n=16)
        tk["x2"][cc] = tx2
        col = 16 * par
        evs = []
        for half in range(2):
            bank = ps[4] if half == 0 else ps[5]
            bfree = tk["r2free"][cc - 1] if cc >= 1 else None
            tm = None
            for k in range(8):
                tm = OP("tensor", lambda e, half=half, k=k, bank=bank, cc=cc: e.matmul(
                    ap(bank), ap(yfm)[:, k * T + cc * 128: k * T + (cc + 1) * 128],
                    ap(woTs)[:, k * D + half * 512: k * D + (half + 1) * 512],
                    start=(k == 0), stop=(k == 7)),
                    waits=[tk["yev"][cc][k], bfree, t_wo] if k == 0 else [],
                    inc="pe" if k == 7 else None)
            tev = OP("vector", lambda e, half=half, bank=bank, col=col, par=par: e.scalar_tensor_tensor(
                r2c[par][:, half * 512:(half + 1) * 512], ap(bank), 1.0,
                xs[par][:, half * 512:(half + 1) * 512], ALU.mult, ALU.add,
                accum_out=ST2[:, col + half: col + half + 1]),
                waits=[tm, tx2, tk["r2cfree"][cc - 2] if cc >= 2 else None], inc="ve")
            evs.append(tev)
        r2ap = r2c[par]
        tspill = OP("sync", lambda e, cc=cc, r2ap=r2ap: e.dma_start(
            out=r2d[cc * 128:(cc + 1) * 128, :], in_=r2ap), waits=evs, inc="so", inc_n=16)
        tqq = OP("scalar", lambda e, r2ap=r2ap, col=col: e.activation(
            ap(sqscr), r2ap, AF.Square, accum_out=ST2[:, col + 2:col + 3]),
            waits=evs, inc="ae")
        tad = OP("vector", lambda e, col=col: e.tensor_add(
            ST2[:, col + 3:col + 4], ST2[:, col:col + 1], ST2[:, col + 1:col + 2]),
            waits=evs, inc="ve")
        tmu = OP("vector", lambda e, col=col: e.tensor_scalar(
            ST2[:, col + 3:col + 4], ST2[:, col + 3:col + 4], 1.0 / D, None, ALU.mult),
            waits=[tad], inc="ve")
        tms = OP("vector", lambda e, col=col: e.tensor_mul(
            ST2[:, col + 4:col + 5], ST2[:, col + 3:col + 4], ST2[:, col + 3:col + 4]),
            waits=[tmu], inc="ve")
        tvr = OP("vector", lambda e, col=col: e.scalar_tensor_tensor(
            ST2[:, col + 5:col + 6], ST2[:, col + 2:col + 3], 1.0 / D,
            ST2[:, col + 4:col + 5], ALU.mult, ALU.subtract), waits=[tms, tqq], inc="ve")
        tsd = OP("scalar", lambda e, col=col: e.activation(
            ST2[:, col + 5:col + 6], ST2[:, col + 5:col + 6], AF.Sqrt, bias=ap(epsc)),
            waits=[tvr, t_ep], inc="ae")
        trs = OP("vector", lambda e, col=col: e.reciprocal(
            ST2[:, col + 6:col + 7], ST2[:, col + 5:col + 6]), waits=[tsd], inc="ve")
        tc2 = OP("vector", lambda e, par=par, r2ap=r2ap, col=col: e.tensor_scalar_sub(
            c2s[par], r2ap, ST2[:, col + 3:col + 4]),
            waits=[tmu, tspill, tk["c2free"][cc - 2] if cc >= 2 else None], inc="ve")
        tam = OP("vector", lambda e, par=par, col=col: e.reduce_max(
            ST2[:, col + 7:col + 8], c2s[par], axis=AX.X, apply_absolute_value=True),
            waits=[tc2], inc="ve")
        tk["r2free"][cc] = [tc2, tqq]
        tk["r2cfree"][cc] = [tqq, tspill]
        tm1 = OP("vector", lambda e, col=col: e.tensor_tensor(
            ST2[:, col + 8:col + 9], ST2[:, col + 6:col + 7], ST2[:, col + 7:col + 8], ALU.mult),
            waits=[trs, tam], inc="ve")
        tm2 = OP("vector", lambda e, col=col: e.tensor_scalar(
            ST2[:, col + 8:col + 9], ST2[:, col + 8:col + 9], 1e-5, None, ALU.max),
            waits=[tm1], inc="ve")
        tdq = OP("vector", lambda e, cc=cc, col=col: e.tensor_scalar(
            ap(deq1c)[:, cc:cc + 1], ST2[:, col + 8:col + 9], 1.0 / 127.0, None, ALU.mult),
            waits=[tm2], inc="ve")
        tmi = OP("vector", lambda e, col=col: e.reciprocal(
            ST2[:, col + 9:col + 10], ST2[:, col + 8:col + 9]), waits=[tm2], inc="ve")
        tu2 = OP("vector", lambda e, col=col: e.scalar_tensor_tensor(
            ST2[:, col + 10:col + 11], ST2[:, col + 6:col + 7], 127.0,
            ST2[:, col + 9:col + 10], ALU.mult, ALU.mult), waits=[tmi], inc="ve")
        tm1a = OP("scalar", lambda e, par=par, col=col: e.activation(
            c2s[par], c2s[par], AF.Identity, scale=ST2[:, col + 10:col + 11],
            bias=ap(mcol)), waits=[tu2, tam, t_mc], inc="ae")
        trn = OP("vector", lambda e, par=par: e.tensor_scalar(
            n1tok[par], c2s[par], MAGIC, None, ALU.subtract),
            waits=[tm1a, tk["n1tr"][cc - 2] if cc >= 2 else None], inc="ve")
        tk["c2free"][cc] = trn
        tk["trn45"][cc] = (trn, par)
        if cc >= 1:
            emit_n1T(cc - 1)
        if cc == n_chunks - 1:
            emit_n1T(cc)

    if debug:
        dbg_n1 = nc.dram_tensor("dbg_n1", [128, 8 * T], BF16, kind="ExternalOutput" if dbg_ext else "Internal").ap()
        dbg_s2 = nc.dram_tensor("dbg_s2", [128, 64], F32, kind="ExternalOutput" if dbg_ext else "Internal").ap()
        OP("sync", lambda e: e.dma_start(out=dbg_n1, in_=ap(n1T)),
           waits=[tk["n1T"][n_chunks - 1]], inc="so", inc_n=16)
        OP("sync", lambda e: e.dma_start(out=dbg_s2[:, 0:16], in_=ap(deq1c)),
           waits=[tk["n1T"][n_chunks - 1]], inc="so", inc_n=16)
        OP("sync", lambda e: e.dma_start(out=dbg_s2[:, 16:20], in_=ap(mqcol)),
           waits=[tk["sw"]], inc="so", inc_n=16)
        OP("sync", lambda e: e.dma_start(out=dbg_s2[:, 20:23], in_=ap(swcol)[:, 0:3]),
           waits=[tk["sw"]], inc="so", inc_n=16)
        ctx.barrier()

    t_ga = OP("vector", lambda e: e.tensor_scalar_mul(ap(gscA), ap(deq1c), ap(mqcol)[:, 0:1]),
              waits=[tk["sw"], tk["n1T"][n_chunks - 1]], inc="ve")
    t_ua = OP("vector", lambda e: e.tensor_scalar_mul(ap(uscA), ap(deq1c), ap(mqcol)[:, 1:2]),
              waits=[t_ga], inc="ve")
    tk["ffnscales"] = t_ua

    ctx.barrier()
    if stop_after == "p45":
        return _finish_stub()

    AMX = ap(amx2)
    prev_out = [[None] * KSUP for _ in range(NSUP)]
    t_final_outs = []
    pdone = {}
    last_pstt = None
    psT_prev = [None]
    n2_prev = {}
    dbank_prev = {}
    wdqg_read = {}
    prev_tmm = [None]
    gulast = {"g": None, "u": None}

    for sup in range(NSUP):
        ccs = list(range(sup * KSUP, (sup + 1) * KSUP))
        pamax = {}
        for g in range(4):
            tqg = []
            tqu = []
            for k in range(8):
                eng, sem = ("scalar", "ad") if k % 2 == 0 else ("gpsimd", "sg")
                tqg.append(OP(eng, lambda e, g=g, k=k: e.dma_start(
                    out=ap(wgq)[:, k * 1024:(k + 1) * 1024],
                    in_=wqgd[k * 128:(k + 1) * 128, g * 1024:(g + 1) * 1024]),
                    waits=[gulast["g"], wq_state["tick"][(0, k * 128, g * 1024)]],
                    inc=sem, inc_n=16))
            for k in range(8):
                eng, sem = ("scalar", "ad") if k % 2 == 1 else ("gpsimd", "sg")
                tqu.append(OP(eng, lambda e, g=g, k=k: e.dma_start(
                    out=ap(wuq)[:, k * 1024:(k + 1) * 1024],
                    in_=wqud[k * 128:(k + 1) * 128, g * 1024:(g + 1) * 1024]),
                    waits=[gulast["u"], wq_state["tick"][(1, k * 128, g * 1024)]],
                    inc=sem, inc_n=16))
            for ti, cc in enumerate(ccs):
                for half in range(2):
                    gbank = ps[half]
                    ubank = ps[2 + half]
                    hpar = half
                    tgm = None
                    for k in range(8):
                        tgm = OP("tensor", lambda e, k=k, half=half, cc=cc, gbank=gbank: e.matmul(
                            ap(gbank), ap(n1T)[:, k * T + cc * 128: k * T + (cc + 1) * 128],
                            ap(wgq)[:, k * 1024 + half * 512: k * 1024 + (half + 1) * 512],
                            start=(k == 0), stop=(k == 7)),
                            waits=[tqg[k], tk["n1T"][cc], pdone.get(("gb", half))]
                            if k == 0 else [tqg[k]],
                            inc="pe" if k == 7 else None)
                    tum = None
                    for k in range(8):
                        tum = OP("tensor", lambda e, k=k, half=half, cc=cc, ubank=ubank: e.matmul(
                            ap(ubank), ap(n1T)[:, k * T + cc * 128: k * T + (cc + 1) * 128],
                            ap(wuq)[:, k * 1024 + half * 512: k * 1024 + (half + 1) * 512],
                            start=(k == 0), stop=(k == 7)),
                            waits=[tqu[k], pdone.get(("ub", half))] if k == 0 else [tqu[k]],
                            inc="pe" if k == 7 else None)
                    tsl = OP("scalar", lambda e, cc=cc, gbank=gbank, hpar=hpar: e.activation(
                        gsilu[hpar], ap(gbank), AF.Silu, scale=ap(gscA)[:, cc:cc + 1]),
                        waits=[tgm, tk["ffnscales"], pdone.get(("ub", half))], inc="ae")
                    pslice_off = ti * INNER + g * 1024 + half * 512
                    tpp = OP("vector", lambda e, cc=cc, ubank=ubank, hpar=hpar, o=pslice_off: e.scalar_tensor_tensor(
                        ap(p_sb)[:, o:o + 512], ap(ubank), ap(uscA)[:, cc:cc + 1],
                        gsilu[hpar], ALU.mult, ALU.mult),
                        waits=[tum, tsl], inc="ve")
                    pdone[("gb", half)] = tsl
                    pdone[("ub", half)] = tpp
                    last_pstt = tpp
                if g == 0:
                    pamax[ti] = OP("vector", lambda e, ti=ti: e.reduce_max(
                        AMX[:, ti:ti + 1],
                        ap(p_sb)[:, ti * INNER: ti * INNER + 1024],
                        axis=AX.X, apply_absolute_value=True),
                        waits=[last_pstt], inc="ve")
                else:
                    OP("vector", lambda e, ti=ti, g=g: e.reduce_max(
                        AMX[:, 12:13],
                        ap(p_sb)[:, ti * INNER + g * 1024: ti * INNER + (g + 1) * 1024],
                        axis=AX.X, apply_absolute_value=True),
                        waits=[last_pstt], inc="ve")
                    pamax[ti] = OP("vector", lambda e, ti=ti: e.tensor_tensor(
                        AMX[:, ti:ti + 1], AMX[:, ti:ti + 1], AMX[:, 12:13],
                        ALU.max), inc="ve")
            gulast["g"] = tgm
            gulast["u"] = tum
        tquant = {}
        for ti, cc in enumerate(ccs):
            tm2 = OP("vector", lambda e, ti=ti: e.tensor_scalar(
                AMX[:, ti:ti + 1], AMX[:, ti:ti + 1], 1e-5, None, ALU.max),
                waits=[pamax[ti]], inc="ve")
            tu3 = OP("vector", lambda e, ti=ti: e.reciprocal(
                AMX[:, 4 + ti:5 + ti], AMX[:, ti:ti + 1]),
                waits=[tm2, prev_tmm[0]], inc="ve")
            tu3b = OP("vector", lambda e, ti=ti: e.tensor_scalar(
                AMX[:, 4 + ti:5 + ti], AMX[:, 4 + ti:5 + ti], 127.0, None, ALU.mult),
                waits=[tu3], inc="ve")
            tds = OP("vector", lambda e, ti=ti: e.tensor_scalar_mul(
                AMX[:, 8 + ti:9 + ti], AMX[:, ti:ti + 1], ap(mqcol)[:, 3:4]),
                waits=[tm2, tk["sw"]], inc="ve")
            tq_hist = [None, None]
            tq = None
            for qtr in range(4):
                qpar = qtr % 2
                tmm = OP("scalar", lambda e, ti=ti, qtr=qtr, qpar=qpar: e.activation(
                    c6s[qpar], ap(p_sb)[:, ti * INNER + qtr * 1024: ti * INNER + (qtr + 1) * 1024],
                    AF.Identity, scale=AMX[:, 4 + ti:5 + ti], bias=ap(mcol)),
                    waits=[tu3b, tq_hist[qpar], t_mc], inc="ae")
                tq = OP("vector", lambda e, ti=ti, qtr=qtr, qpar=qpar: e.tensor_scalar(
                    ap(p_sb)[:, ti * INNER + qtr * 1024: ti * INNER + (qtr + 1) * 1024],
                    c6s[qpar], MAGIC, None, ALU.subtract),
                    waits=[tmm], inc="ve")
                tq_hist[qpar] = tq
            tquant[ti] = (tq, tds)
        prev_tmm[0] = tmm
        tracc = {}
        for ti, cc in enumerate(ccs):
            tracc[ti] = OP("sync", lambda e, ti=ti, cc=cc: e.dma_start(
                out=ap(racc)[:, ti * 1024:(ti + 1) * 1024], in_=r2d[cc * 128:(cc + 1) * 128, :]),
                waits=[prev_out[sup - 1][ti] if sup >= 1 else None], inc="sd", inc_n=16)
        for kg in range(4):
            kpar = kg % 2
            wldds = []
            for k8 in range(8):
                eng, sem = ("scalar", "ad") if k8 % 2 == 0 else ("gpsimd", "sg")
                wl_waits = []
                if k8 <= 1:
                    wl_waits = [wdqg_read.get(kpar)]
                if sup == 0:
                    wl_waits = wl_waits + [wq_state["tick"][(2, (kg * 8 + k8) * 128, 0)]]
                wldds.append(OP(eng, lambda e, kg=kg, k8=k8, kpar=kpar: e.dma_start(
                    out=wdqg[kpar][:, k8 * 1024:(k8 + 1) * 1024],
                    in_=wqdd[(kg * 8 + k8) * 128:(kg * 8 + k8 + 1) * 128, :]),
                    waits=wl_waits, inc=sem, inc_n=16))
            tevs = []
            for ti, cc in enumerate(ccs):
                tpar = ti % 2
                tT = None
                for j in range(8):
                    tT = OP("tensor", lambda e, ti=ti, kg=kg, j=j: e.transpose(
                        ap(psT)[:, j * 128:(j + 1) * 128],
                        ap(p_sb)[:, ti * INNER + kg * 1024 + j * 128: ti * INNER + kg * 1024 + (j + 1) * 128],
                        ap(ident)), waits=[tquant[ti][0], psT_prev[0], n2_prev.get(tpar)]
                        if j == 0 else [], inc="pe" if j == 7 else None)
                tEv = None
                for j in range(8):
                    tEv = OP("vector", lambda e, j=j, tpar=tpar: e.tensor_scalar(
                        n2Ts[tpar][:, j * 128:(j + 1) * 128],
                        ap(psT)[:, j * 128:(j + 1) * 128], -128.0, 127.0, ALU.max, ALU.min),
                        waits=[tT] if j == 0 else [], inc="ve" if j == 7 else None)
                for half in range(2):
                    dbank = ps[4 + half]
                    tdm = None
                    for k8 in range(8):
                        tdm = OP("tensor", lambda e, k8=k8, half=half, tpar=tpar, kpar=kpar, dbank=dbank: e.matmul(
                            ap(dbank), n2Ts[tpar][:, k8 * 128:(k8 + 1) * 128],
                            wdqg[kpar][:, k8 * 1024 + half * 512: k8 * 1024 + (half + 1) * 512],
                            start=(k8 == 0), stop=(k8 == 7)),
                            waits=[tEv, wldds[k8], tracc[ti], dbank_prev.get(half)]
                            if k8 == 0 else [wldds[k8]], inc="pe" if k8 == 7 else None)
                    tev = OP("vector", lambda e, ti=ti, half=half, dbank=dbank: e.scalar_tensor_tensor(
                        ap(racc)[:, ti * 1024 + half * 512: ti * 1024 + (half + 1) * 512],
                        ap(dbank), AMX[:, 8 + ti:9 + ti],
                        ap(racc)[:, ti * 1024 + half * 512: ti * 1024 + (half + 1) * 512],
                        ALU.mult, ALU.add), waits=[tdm, tquant[ti][1]], inc="ve")
                    tevs.append(tev)
                    n2_prev[tpar] = tdm
                    dbank_prev[half] = tev
            wdqg_read[kpar] = n2_prev[(KSUP - 1) % 2]
            if kg == 3:
                for ti, cc in enumerate(ccs):
                    tout = OP("sync", lambda e, ti=ti, cc=cc: e.dma_start(
                        out=outd[cc * 128:(cc + 1) * 128, :],
                        in_=ap(racc)[:, ti * 1024:(ti + 1) * 1024]),
                        waits=[tevs[2 * ti], tevs[2 * ti + 1]], inc="so", inc_n=16)
                    prev_out[sup][ti] = tout
                    t_final_outs.append(tout)

    OP("sync", lambda e: e.nop(), waits=[t_final_outs[-1],
                                         ("so", ctx.cnt.get("so", 0)),
                                         ("sd", ctx.cnt.get("sd", 0)),
                                         ("sg", ctx.cnt.get("sg", 0)),
                                         ("ad", ctx.cnt.get("ad", 0))])

    build._last_ctx = ctx
    build._last_tk = tk
    with nc.Block() as block:
        ctx.emit_all(block)
    return nc


_CACHE = {}


def kernel(L0_attn_x, L0_attn_W_qkv, L0_attn_W_o, L0_attn_log_gamma,
           L0_ffn_w_gate, L0_ffn_w_up, L0_ffn_w_down):
    x = np.ascontiguousarray(np.asarray(L0_attn_x, dtype=np.float32).reshape(NCORES * T, D))
    wqkvT = np.ascontiguousarray(np.asarray(L0_attn_W_qkv, dtype=np.float32).T)
    woT = np.ascontiguousarray(np.asarray(L0_attn_W_o, dtype=np.float32).T)
    wgT = np.ascontiguousarray(np.asarray(L0_ffn_w_gate, dtype=np.float32).T)
    wuT = np.ascontiguousarray(np.asarray(L0_ffn_w_up, dtype=np.float32).T)
    wdT = np.ascontiguousarray(np.asarray(L0_ffn_w_down, dtype=np.float32).T)
    lg = np.ascontiguousarray(np.asarray(L0_attn_log_gamma, dtype=np.float32).reshape(1, H))

    if "nc" not in _CACHE:
        _CACHE["nc"] = build()
    nc = _CACHE["nc"]

    in_maps = []
    for c in range(NCORES):
        wsl = np.ascontiguousarray(np.concatenate([
            wgT[c * 128:(c + 1) * 128, :],
            wuT[c * 128:(c + 1) * 128, :],
            wdT[c * 512:(c + 1) * 512, :].reshape(128, INNER)], axis=0))
        in_maps.append({
            "x": x[c * T:(c + 1) * T],
            "wqkvT": wqkvT, "woT": woT, "wgT": wgT, "wuT": wuT, "wdT": wdT,
            "lg": lg, "wsl": wsl,
        })
    res = run_bass_kernel_spmd(nc, in_maps, core_ids=list(range(NCORES)))
    out = np.concatenate([r["out"] for r in res.results], axis=0)
    return out.reshape(1, NCORES * T, D).astype(np.float32)


# version 23 (final)
# speedup vs baseline: 1.0207x; 1.0207x over previous
import numpy as np
import ml_dtypes
import concourse.bass as bass
import concourse.mybir as mybir
from concourse.bass_utils import run_bass_kernel_spmd

F32 = mybir.dt.float32
BF16 = mybir.dt.bfloat16
U32 = mybir.dt.uint32
AF = mybir.ActivationFunctionType
ALU = mybir.AluOpType
AX = mybir.AxisListType

NCORES = 8
T = 2048
C = 128
NCH = T // C
H = 16
DK = 64
D = 1024
INNER = 4096
LN_EPS = 1e-5
MAGIC = 12582912.0
KSUP = 4
NSUP = NCH // KSUP
ENGINES = ("sync", "tensor", "vector", "scalar", "gpsimd")


def _flat(waits):
    out = []
    for w in waits:
        if w is None:
            continue
        if isinstance(w, list):
            out.extend(x for x in w if x is not None)
        else:
            out.append(w)
    return out


class Ctx:
    def __init__(self, nc):
        self.nc = nc
        self.recs = {e: [] for e in ENGINES}
        self.sems = {}
        self.cnt = {}
        self.pending_barrier = {e: None for e in ENGINES}

    def sem(self, name):
        if name not in self.sems:
            self.sems[name] = self.nc.alloc_semaphore(name)
            self.cnt[name] = 0
        return self.sems[name]

    def op(self, eng, fn, waits=(), inc=None, inc_n=1):
        tick = None
        if inc is not None:
            self.sem(inc)
            self.cnt[inc] += inc_n
            tick = (inc, self.cnt[inc])
        ws = _flat(waits)
        if self.pending_barrier[eng]:
            ws = ws + self.pending_barrier[eng]
            self.pending_barrier[eng] = None
        for (sn, sv) in ws:
            self.sem(sn)
        self.recs[eng].append((fn, tuple(ws), tick, inc_n))
        return tick

    def barrier(self):
        snap = [(sn, self.cnt[sn]) for sn in self.sems]
        for e in ENGINES:
            prev = self.pending_barrier[e] or []
            self.pending_barrier[e] = prev + snap
        return snap

    def emit_all(self, block):
        def mk(engname):
            recs = self.recs[engname]
            sems = self.sems

            def emitter(eng):
                waited = {}
                for fn, waits, tick, inc_n in recs:
                    for (sn, sv) in waits:
                        if waited.get(sn, -1) < sv:
                            eng.wait_ge(sems[sn], sv)
                            waited[sn] = sv
                    inst = fn(eng)
                    if tick is not None:
                        inst.then_inc(sems[tick[0]], inc_n)
            return emitter

        block.sync(mk("sync"))
        block.tensor(mk("tensor"))
        block.vector(mk("vector"))
        block.scalar(mk("scalar"))
        block.gpsimd(mk("gpsimd"))


def bc(ap2d, blk):
    return ap2d.unsqueeze(2).broadcast_to([ap2d.shape[0], ap2d.shape[1], blk])


def seg(ap2d, blk):
    return ap2d.rearrange("p (a b) -> p a b", b=blk)


class DBuf:
    def __init__(self, t, w):
        self.t, self.w = t, w

    def __getitem__(self, par):
        return self.t.ap()[:, par * self.w:(par + 1) * self.w]


def build(n_chunks=NCH, debug=False, stop_after=None, skip=(), dbg_ext=False):
    nc = bass.Bass("TRN2", target_bir_lowering=False, detect_race_conditions=False)

    xd = nc.dram_tensor("x", [T, D], F32, kind="ExternalInput").ap()
    wqkvTd = nc.dram_tensor("wqkvT", [D, 3 * D], F32, kind="ExternalInput").ap()
    woTd = nc.dram_tensor("woT", [D, D], F32, kind="ExternalInput").ap()
    wgTd = nc.dram_tensor("wgT", [D, INNER], F32, kind="ExternalInput").ap()
    wuTd = nc.dram_tensor("wuT", [D, INNER], F32, kind="ExternalInput").ap()
    wdTd = nc.dram_tensor("wdT", [INNER, D], F32, kind="ExternalInput").ap()
    lgd = nc.dram_tensor("lg", [1, H], F32, kind="ExternalInput").ap()
    wsld = nc.dram_tensor("wsl", [3 * 128, INNER], F32, kind="ExternalInput").ap()
    outd = nc.dram_tensor("out", [T, D], F32, kind="ExternalOutput").ap()

    r2d = nc.dram_tensor("r2d", [T, D], F32, kind="ExternalOutput" if dbg_ext else "Internal").ap()
    sprefd = nc.dram_tensor("sprefd", [NCH * 128, 512], BF16).ap()
    wqgd = nc.dram_tensor("wqgd", [D, INNER], BF16).ap()
    wqud = nc.dram_tensor("wqud", [D, INNER], BF16).ap()
    wqdd = nc.dram_tensor("wqdd", [INNER, D], BF16).ap()
    ccin = nc.dram_tensor("ccin", [129, 512], F32).ap()
    ccout = nc.dram_tensor("ccout", [NCORES * 129, 512], F32).ap()
    ccin1 = nc.dram_tensor("ccin1", [1, 8], F32).ap()
    ccout1 = nc.dram_tensor("ccout1", [NCORES, 8], F32).ap()

    identd = nc.inline_tensor(np.eye(128, dtype=ml_dtypes.bfloat16), "identc")
    onesrd = nc.inline_tensor(np.ones((1, 128), dtype=np.float32), "onesrow")

    ctx = Ctx(nc)
    OP = ctx.op

    base, top = nc.bump_sbuf(207 * 1024)

    cur = [base]

    def At(name, shape, dtype, at=None):
        if at is None:
            at = cur[0]
            nbytes = int(np.prod(shape[1:])) * mybir.dt.size(dtype)
            cur[0] = (at + nbytes + 63) & ~63
            assert cur[0] <= base + 34 * 1024, f"persistent overflow at {name}"
        return nc.alloc_sbuf_tensor_at(name, shape, dtype, offset=at)

    ident = At("ident", [128, 128], BF16)
    onesr = At("onesr", [1, 128], F32)
    onesc = At("onesc", [128, 1], F32)
    lgrow = At("lgrow", [1, H], F32)
    lgcol = At("lgcol", [128, H], F32)
    lgstk = At("lgstk", [128, 8], F32)
    gq16 = At("gq16", [128, H], F32)
    gkv16 = At("gkv16", [128, H], F32)
    gcu = At("gcu", [128, 8], F32)
    gbl = At("gbl", [128, 8], F32)
    gbm = At("gbm", [128, 64], F32)
    tcc_off = cur[0]
    tcc = At("tcc", [128, 128], F32)
    m01 = At("m01", [128, 128], BF16)
    dmask = At("dmask", [128, H * 128], F32)
    icol = At("icol", [128, 1], F32)
    i2col = At("i2col", [128, 1], F32)
    pidu = At("pidu", [1, 1], U32)
    pidf = At("pidf", [1, 1], F32)
    jrow = At("jrow", [1, 8], F32)
    erow = At("erow", [1, 8], F32)
    birow = At("birow", [1, 8], F32)
    ebcol = At("ebcol", [128, 16], F32)
    wsrow = At("wsrow", [1, 32], F32)
    swcol = At("swcol", [128, 4], F32)
    mqcol = At("mqcol", [128, 4], F32)
    stats = At("stats", [128, 32], F32)
    kst = At("kst", [128, 96], F32)
    st2 = At("st2", [128, 32], F32)
    deq1c = At("deq1c", [128, NCH], F32)
    gscA = At("gscA", [128, NCH], F32)
    uscA = At("uscA", [128, NCH], F32)
    amx2 = At("amx2", [128, 16], F32)
    mcol = At("mcol", [128, 1], F32, at=tcc_off)
    negm = At("negm", [128, 1], F32, at=tcc_off + 64)
    wpart = At("wpart", [128, 4], F32)
    epsc = At("epsc", [128, 1], F32)
    sqscr = At("sqscr", [128, D], F32)
    s32_off = cur[0]
    S32 = At("S32", [128, 512], F32)
    SINI = At("SINI", [128, 512], F32)
    sfull_off = cur[0]
    sfull = DBuf(At("sfull", [128, 2 * 1024], BF16), 1024)
    qstg5 = At("qstg5", [128, D], F32, at=s32_off)
    qstg6 = At("qstg6", [128, D], F32, at=sfull_off)
    i1s1 = At("i1s1", [128, 512], F32, at=sfull_off)
    wqo1 = DBuf(At("wqo1", [128, 2 * 512], BF16, at=sfull_off + 2048), 512)
    sprefl = DBuf(At("sprefl", [128, 2 * 512], BF16), 512)
    xs = DBuf(At("xs", [128, 2 * D], F32), D)

    RA = base + 34 * 1024
    V_tm = At("V_tm", [128, NCH * D], BF16, at=RA)
    q_tm = At("q_tm", [128, NCH * D], BF16, at=RA + 32 * 1024)
    khat_fm = At("khat_fm", [128, 8 * T], BF16, at=RA + 64 * 1024)
    n1T = At("n1T", [128, 8 * T], BF16, at=RA + 64 * 1024)
    p_sb = At("p_sb", [128, KSUP * INNER], BF16, at=RA)
    wgq = At("wgq", [128, 8 * 1024], BF16, at=RA + 32 * 1024)
    wuq = At("wuq", [128, 8 * 1024], BF16, at=RA + 48 * 1024)

    RB = base + 130 * 1024
    h1s = DBuf(At("h1s", [128, 2 * D], BF16, at=RB), D)
    h1T = DBuf(At("h1T", [128, 2 * D], BF16, at=RB + 4 * 1024), D)
    ktmp = DBuf(At("ktmp", [128, 2 * 512], F32, at=RB + 8 * 1024), 512)
    khtm = DBuf(At("khtm", [128, 2 * D], BF16, at=RB + 12 * 1024), D)
    ktil = DBuf(At("ktil", [128, 2 * D], BF16, at=RB + 16 * 1024), D)
    qts = DBuf(At("qts", [128, 2 * D], BF16, at=RB), D)
    qpad2 = At("qpad2", [128, 2048], BF16, at=RB + 4 * 1024)
    qtpad2 = At("qtpad2", [128, 2048], BF16, at=RB + 8 * 1024)
    asb = DBuf(At("asb", [128, 2 * 256], BF16, at=RB + 12 * 1024), 256)
    i1s = At("i1s", [128, D], F32, at=RB + 13 * 1024)
    wqo = DBuf(At("wqo", [128, 2 * D], BF16, at=RB + 17 * 1024), D)
    r2c = DBuf(At("r2c", [128, 2 * D], F32, at=RB), D)
    c2s = DBuf(At("c2s", [128, 2 * D], F32, at=RB + 8 * 1024), D)
    n1tok = DBuf(At("n1tok", [128, 2 * D], BF16, at=RB + 16 * 1024), D)
    gsilu = DBuf(At("gsilu", [128, 2 * 512], F32, at=RB), 512)
    n2Ts = DBuf(At("n2Ts", [128, 2 * D], BF16, at=RB + 4 * 1024), D)
    c6s = DBuf(At("c6s", [128, 2 * D], F32, at=RB + 8 * 1024), D)
    q6s = At("q6s", [128, D], F32, at=RB + 16 * 1024)

    RC = base + 151 * 1024
    wkv = At("wkv", [128, 8 * 3 * D], BF16, at=RC)
    wstg = DBuf(At("wstg", [128, 2 * D], F32, at=RC + 48 * 1024), D)
    sg8 = At("sg8", [128, 8 * 512], F32, at=RC)
    ws8 = At("ws8", [8, 512], F32, at=RC + 16 * 1024)
    yfm = At("yfm", [128, 8 * T], BF16, at=RC)
    woTs = At("woTs", [128, 8 * D], BF16, at=RC + 32 * 1024)
    wdqg = DBuf(At("wdqg", [128, 2 * 8 * 1024], BF16, at=RC), 8 * 1024)
    racc = At("racc", [128, KSUP * D], F32, at=RC + 32 * 1024)

    assert RC + 56 * 1024 <= top, (RC, top)

    ps = [nc.alloc_psum_tensor(f"ps{i}", [128, 512], F32) for i in range(6)]
    psT = nc.alloc_psum_tensor("psT", [128, 1024], BF16)
    ps6 = nc.alloc_psum_tensor("ps6", [128, 512], F32)

    def ap(t):
        return t.ap()

    tk = {}

    def _finish_stub():
        OP("sync", lambda e: e.dma_start(out=outd[0:128, :], in_=xs[0]), inc="so", inc_n=16)
        OP("sync", lambda e: e.nop(), waits=[("so", ctx.cnt.get("so", 0)),
                                             ("sd", ctx.cnt.get("sd", 0)),
                                             ("sg", ctx.cnt.get("sg", 0))])
        with nc.Block() as block:
            ctx.emit_all(block)
        return nc


    t_id = OP("sync", lambda e: e.dma_start(out=ap(ident), in_=identd.ap()), inc="sd", inc_n=16)
    t_or = OP("sync", lambda e: e.dma_start(out=ap(onesr), in_=onesrd.ap()), inc="sd", inc_n=16)
    t_lg = OP("sync", lambda e: e.dma_start(out=ap(lgrow), in_=lgd), inc="sd", inc_n=16)
    t_pid = OP("sync", lambda e: e.dma_start(out=ap(pidu), in_=nc.partition_id_tensor.ap()),
               inc="sd", inc_n=16)
    t_wkv = None
    for k in range(8):
        t_wkv = OP("gpsimd", lambda e, k=k: e.dma_start(
            out=ap(wkv)[:, k * 3 * D:(k + 1) * 3 * D], in_=wqkvTd[k * 128:(k + 1) * 128, :]),
            inc="sg", inc_n=16)
    t1 = OP("scalar", lambda e: e.activation(ap(lgrow), ap(lgrow), AF.Sigmoid),
            waits=[t_pid], inc="ae")
    t2 = OP("scalar", lambda e: e.activation(ap(lgrow), ap(lgrow), AF.Ln), waits=[t1], inc="ae")
    t_pf = OP("vector", lambda e: e.tensor_copy(ap(pidf), ap(pidu)), waits=[t_pid], inc="ve")
    t_oc = OP("vector", lambda e: e.memset(ap(onesc), 1.0), inc="ve")
    t_ep = OP("vector", lambda e: e.memset(ap(epsc), LN_EPS), inc="ve")
    OP("vector", lambda e: e.memset(ap(kst), 0.0), inc="ve")

    t3 = OP("tensor", lambda e: e.matmul(ap(ps6)[:, 0:H], ap(onesr), ap(lgrow),
                                         start=True, stop=True), waits=[t2, t_pid], inc="pe")
    t4 = OP("vector", lambda e: e.tensor_copy(ap(lgcol), ap(ps6)[:, 0:H]), waits=[t3], inc="ve")
    t5 = None
    for hp in range(8):
        for s in range(2):
            t5 = OP("vector", lambda e, hp=hp, s=s: e.tensor_copy(
                ap(lgstk)[64 * s:64 * s + 64, hp:hp + 1],
                ap(lgcol)[64 * s:64 * s + 64, 2 * hp + s:2 * hp + s + 1]),
                waits=[t4], inc="ve")

    t6 = OP("gpsimd", lambda e: e.iota(ap(tcc), pattern=[[1, 128]], base=0,
                                       channel_multiplier=-1,
                                       allow_small_or_imprecise_dtypes=True), inc="ge")
    t7 = OP("gpsimd", lambda e: e.iota(ap(icol), pattern=[[1, 1]], base=0,
                                       channel_multiplier=1,
                                       allow_small_or_imprecise_dtypes=True), inc="ge")
    t8 = OP("gpsimd", lambda e: e.iota(ap(i2col), pattern=[[1, 1]], base=127,
                                       channel_multiplier=-1,
                                       allow_small_or_imprecise_dtypes=True), inc="ge")
    t9 = OP("gpsimd", lambda e: e.iota(ap(jrow), pattern=[[1, 8]], base=0,
                                       channel_multiplier=0,
                                       allow_small_or_imprecise_dtypes=True), inc="ge")
    t10 = OP("vector", lambda e: e.tensor_scalar(ap(m01), ap(tcc), 0.0, None, ALU.is_ge),
             waits=[t6], inc="ve")
    t11 = OP("vector", lambda e: e.tensor_scalar(ap(tcc), ap(tcc), -1.0, None, ALU.max),
             waits=[t10], inc="ve")
    t_dm = None
    for h in range(H):
        ta = OP("scalar", lambda e, h=h: e.activation(
            ap(sqscr)[:, 0:128], ap(tcc), AF.Exp, scale=ap(lgcol)[:, h:h + 1]),
            waits=[t11, t4, t_dm], inc="ae")
        t_dm = OP("vector", lambda e, h=h: e.tensor_mul(
            ap(dmask)[:, h * 128:(h + 1) * 128], ap(sqscr)[:, 0:128], ap(m01)),
            waits=[ta], inc="ve")
    t_mc = OP("vector", lambda e: e.memset(ap(mcol), MAGIC), waits=[t_dm], inc="ve")
    t_nm = OP("vector", lambda e: e.memset(ap(negm), -MAGIC), waits=[t_mc], inc="ve")
    t_gq = None
    for h in range(H):
        t_gq = OP("scalar", lambda e, h=h: e.activation(
            ap(gq16)[:, h:h + 1], ap(icol), AF.Exp,
            scale=ap(lgcol)[:, h:h + 1], bias=ap(lgcol)[:, h:h + 1]),
            waits=[t7, t4], inc="ae")
    t_gkv = None
    for h in range(H):
        t_gkv = OP("scalar", lambda e, h=h: e.activation(
            ap(gkv16)[:, h:h + 1], ap(i2col), AF.Exp, scale=ap(lgcol)[:, h:h + 1]),
            waits=[t8, t4], inc="ae")
    t_gcu = OP("scalar", lambda e: e.activation(ap(gcu), ap(lgstk), AF.Exp, scale=128.0),
               waits=[t5], inc="ae")
    t_gbl = OP("vector", lambda e: e.tensor_scalar(ap(gbl), ap(lgstk), 2048.0, None, ALU.mult),
               waits=[t5], inc="ve")
    tc1 = OP("vector", lambda e: e.tensor_scalar(ap(pidf), ap(pidf), -1.0, None, ALU.add),
             waits=[t_pf], inc="ve")
    te = OP("vector", lambda e: e.tensor_scalar(ap(erow), ap(jrow), -1.0, ap(pidf)[0:1, 0:1],
                                                ALU.mult, ALU.add), waits=[t9, tc1], inc="ve")
    tind = OP("vector", lambda e: e.tensor_single_scalar(ap(birow), ap(jrow),
                                                         ap(pidf)[0:1, 0:1], ALU.is_le),
              waits=[t9, tc1], inc="ve")
    tbi = OP("vector", lambda e: e.tensor_scalar(ap(birow), ap(birow), -1.0, 1e30,
                                                 ALU.add, ALU.mult), waits=[tind], inc="ve")
    teb = OP("tensor", lambda e: e.matmul(ap(ps6)[:, 0:8], ap(onesr), ap(erow),
                                          start=True, stop=True), waits=[te, t4], inc="pe")
    teb2 = OP("tensor", lambda e: e.matmul(ap(ps6)[:, 8:16], ap(onesr), ap(birow),
                                           start=True, stop=True), waits=[tbi], inc="pe")
    tebv = OP("vector", lambda e: e.tensor_copy(ap(ebcol), ap(ps6)[:, 0:16]),
              waits=[teb, teb2], inc="ve")
    t_gb = None
    for j in range(8):
        t_gb = OP("scalar", lambda e, j=j: e.activation(
            ap(gbm)[:, j * 8:(j + 1) * 8], ap(gbl), AF.Exp,
            scale=ap(ebcol)[:, j:j + 1], bias=ap(ebcol)[:, 8 + j:9 + j]),
            waits=[tebv, t_gbl], inc="ae")

    if stop_after == "p0":
        ctx.barrier()
        return _finish_stub()

    t_ws_last = []
    for mi in range(3):
        tb = None
        prev_dma = [None, None]
        first = True
        for c0 in range(0, INNER, 1024):
            par = (c0 // 1024) % 2
            td = OP("sync", lambda e, mi=mi, c0=c0, par=par: e.dma_start(
                out=wstg[par], in_=wsld[mi * 128:(mi + 1) * 128, c0:c0 + 1024]),
                waits=[prev_dma[par]], inc="sd", inc_n=16)
            ta = OP("scalar", lambda e, par=par: e.activation(
                ap(sqscr), wstg[par], AF.Abs, accum_out=ap(wpart)[:, 3:4]),
                waits=[td, t_dm], inc="ae")
            if first:
                tb = OP("vector", lambda e, mi=mi: e.tensor_copy(
                    ap(wpart)[:, mi:mi + 1], ap(wpart)[:, 3:4]), waits=[ta], inc="ve")
                first = False
            else:
                tb = OP("vector", lambda e, mi=mi: e.tensor_add(
                    ap(wpart)[:, mi:mi + 1], ap(wpart)[:, mi:mi + 1], ap(wpart)[:, 3:4]),
                    waits=[ta], inc="ve")
            prev_dma[par] = tb
        t_ws_last.append(tb)
    t_wsum = OP("tensor", lambda e: e.matmul(ap(ps6)[0:1, 0:3], ap(onesc),
                                             ap(wpart)[:, 0:3], start=True, stop=True),
                waits=t_ws_last + [t_oc, tebv], inc="pe")
    t_wsv = OP("vector", lambda e: e.tensor_copy(ap(stats)[0:1, 28:31], ap(ps6)[0:1, 0:3]),
               waits=[t_wsum], inc="ve")
    t_wz = OP("vector", lambda e: e.memset(ap(stats)[0:1, 8:13], 0.0), inc="ve")
    t_cw1 = OP("sync", lambda e: e.dma_start(out=ccin1[0:1, 0:3], in_=ap(stats)[0:1, 28:31]),
               waits=[t_wsv], inc="so", inc_n=16)
    t_cz1 = OP("sync", lambda e: e.dma_start(out=ccin1[0:1, 3:8], in_=ap(stats)[0:1, 8:13]),
               waits=[t_wz], inc="so", inc_n=16)
    ctx.barrier()
    t_cc1 = OP("gpsimd", lambda e: e.collective_compute(
        "AllGather", ALU.bypass, replica_groups=[list(range(NCORES))],
        ins=[ccin1.opt()], outs=[ccout1.opt()]), waits=[t_cw1, t_cz1], inc="ge")
    if stop_after == "p0w":
        return _finish_stub()

    wq_pairs = []
    for gp in (0, 2):
        for mi, (wsrc, wdst) in ((0, (wgTd, wqgd)), (1, (wuTd, wqud))):
            for r in range(0, D, 128):
                wq_pairs.append((mi, wsrc, wdst, r, gp * 1024, "cols"))
        if gp == 0:
            for kg2 in range(8):
                wq_pairs.append((2, wdTd, wqdd, kg2 * 256, 0, "rows"))
    for kg2 in range(8, 16):
        wq_pairs.append((2, wdTd, wqdd, kg2 * 256, 0, "rows"))
    wq_state = {"i": 0, "po": [None, None], "pi": [None, None], "tick": {}}
    wq_pairbufs = [wstg, xs]

    def wq_emit(npairs):
        for _ in range(npairs):
            i = wq_state["i"]
            if i >= len(wq_pairs):
                return
            wq_state["i"] += 1
            mi, wsrc, wdst, r, c0, kind = wq_pairs[i]
            eng, sem = ("scalar", "ad") if i % 2 == 0 else ("gpsimd", "sg")
            par = i % 2
            pbuf = wq_pairbufs[par]
            if kind == "cols":
                src = wsrc[r:r + 128, c0:c0 + 2048]
            else:
                src = wsrc[r:r + 256, :].rearrange("(p a) c -> p a c", p=128)
            tdl = OP(eng, lambda e, src=src, pbuf=pbuf: e.dma_start(
                out=pbuf.t.ap(), in_=src),
                waits=[wq_state["pi"][par], tk["sw"]], inc=sem, inc_n=16)
            ta = None
            for h in range(2):
                opar = h
                ta = OP("vector", lambda e, mi=mi, pbuf=pbuf, h=h: e.tensor_scalar(
                    ap(i1s), pbuf[h], ap(swcol)[:, mi:mi + 1], -1.0, ALU.mult, ALU.max),
                    waits=[tdl], inc="ve")
                tb = OP("vector", lambda e: e.tensor_scalar(
                    ap(i1s), ap(i1s), 1.0, MAGIC, ALU.min, ALU.add),
                    waits=[ta], inc="ve")
                tiq = OP("vector", lambda e, opar=opar: e.tensor_scalar(
                    wqo[opar], ap(i1s), MAGIC, None, ALU.subtract),
                    waits=[tb, wq_state["po"][opar]], inc="ve")
                if kind == "cols":
                    dst = wdst[r:r + 128, c0 + h * 1024:c0 + (h + 1) * 1024]
                    key = (mi, r, c0 + h * 1024)
                else:
                    dst = wdst[r:r + 256, :].rearrange("(p a) c -> p a c", p=128)[:, h, :]
                    key = (mi, r + h, c0)
                tdo = OP("sync", lambda e, dst=dst, opar=opar: e.dma_start(
                    out=dst, in_=wqo[opar]), waits=[tiq], inc="so", inc_n=16)
                wq_state["po"][opar] = tdo
                wq_state["tick"][key] = tdo
            wq_state["pi"][par] = ta
            if kind == "rows":
                wq_state["tick"][(2, r, 0)] = tdo
                wq_state["tick"][(2, r + 128, 0)] = tdo

    tk["xfree"] = [None] * n_chunks
    tk["qkv_ev"] = [[None] * 6 for _ in range(n_chunks)]
    tk["mmn"] = [[None] * 6 for _ in range(n_chunks)]
    tk["vready"] = [None] * n_chunks
    tk["q_ev"] = [None] * n_chunks
    tk["khfm"] = [None] * n_chunks
    tk["spref"] = [None] * n_chunks
    tk["schain"] = [None] * n_chunks
    tk["psT_free"] = [None] * n_chunks
    tk["ksq_rd"] = [None] * n_chunks
    tk["h1tr"] = [None] * n_chunks
    tk["delta"] = [None] * n_chunks
    tk["ktilg"] = [None] * n_chunks
    tk["sprefdma"] = [None] * n_chunks
    STATS = ap(stats)
    KST = ap(kst)

    t_s0 = OP("vector", lambda e: e.memset(ap(S32), 0.0), inc="ve")
    tk["schain_prev"] = t_s0

    for cc in range(n_chunks):
        if cc == 8:
            t_rb1 = OP("sync", lambda e: e.dma_start(
                out=ap(stats)[0:8, 8:16], in_=ccout1), waits=[t_cc1], inc="sd", inc_n=16)
            t_m1 = OP("tensor", lambda e: e.matmul(
                ap(ps6)[0:1, 8:16], ap(onesc)[0:8, 0:1], ap(stats)[0:8, 8:16],
                start=True, stop=True), waits=[t_rb1, t_oc], inc="pe")
            t_w2 = OP("vector", lambda e: e.tensor_scalar(
                ap(wsrow)[0:1, 0:3], ap(ps6)[0:1, 8:11], 1.0 / (D * INNER), 1e-5,
                ALU.mult, ALU.max), waits=[t_m1], inc="ve")
            t_w3 = OP("tensor", lambda e: e.matmul(
                ap(ps6)[:, 0:3], ap(onesr), ap(wsrow)[0:1, 0:3],
                start=True, stop=True), waits=[t_w2], inc="pe")
            t_w4 = OP("vector", lambda e: e.reciprocal(ap(swcol)[:, 0:3], ap(ps6)[:, 0:3]),
                      waits=[t_w3], inc="ve")
            t_w5 = OP("vector", lambda e: e.tensor_copy(ap(mqcol)[:, 0:3], ap(ps6)[:, 0:3]),
                      waits=[t_w3], inc="ve")
            t_w6 = OP("vector", lambda e: e.tensor_scalar(
                ap(mqcol)[:, 3:4], ap(mqcol)[:, 2:3], 1.0 / 127.0, None, ALU.mult),
                waits=[t_w5], inc="ve")
            tk["sw"] = [t_w4, t_w5, t_w6]
        par = cc % 2
        xpap = xs[par]
        w_free = tk["xfree"][cc - 2] if cc >= 2 else None
        txd = OP("sync", lambda e, cc=cc, xpap=xpap: e.dma_start(
            out=xpap, in_=xd[cc * 128:(cc + 1) * 128, :]), waits=[w_free], inc="sd", inc_n=16)
        col = 16 * par
        ts1 = OP("vector", lambda e, xpap=xpap, col=col: e.reduce_sum(
            STATS[:, col:col + 1], xpap, axis=AX.X), waits=[txd], inc="ve")
        ts2 = OP("scalar", lambda e, xpap=xpap, col=col: e.activation(
            ap(sqscr), xpap, AF.Square, accum_out=STATS[:, col + 1:col + 2]),
            waits=[txd, tk["ksq_rd"][cc - 1] if cc >= 1 else None], inc="ae")
        tmu = OP("vector", lambda e, col=col: e.tensor_scalar(
            STATS[:, col + 2:col + 3], STATS[:, col:col + 1], 1.0 / D, None, ALU.mult),
            waits=[ts1], inc="ve")
        tms = OP("vector", lambda e, col=col: e.tensor_mul(
            STATS[:, col + 3:col + 4], STATS[:, col + 2:col + 3], STATS[:, col + 2:col + 3]),
            waits=[tmu], inc="ve")
        tvr = OP("vector", lambda e, col=col: e.scalar_tensor_tensor(
            STATS[:, col + 4:col + 5], STATS[:, col + 1:col + 2], 1.0 / D,
            STATS[:, col + 3:col + 4], ALU.mult, ALU.subtract), waits=[tms, ts2], inc="ve")
        tsd = OP("scalar", lambda e, col=col: e.activation(
            STATS[:, col + 5:col + 6], STATS[:, col + 4:col + 5], AF.Sqrt, bias=ap(epsc)),
            waits=[tvr, t_ep], inc="ae")
        trs = OP("vector", lambda e, col=col: e.reciprocal(
            STATS[:, col + 6:col + 7], STATS[:, col + 5:col + 6]), waits=[tsd], inc="ve")
        tnm = OP("vector", lambda e, col=col: e.scalar_tensor_tensor(
            STATS[:, col + 7:col + 8], STATS[:, col + 2:col + 3], -1.0,
            STATS[:, col + 6:col + 7], ALU.mult, ALU.mult), waits=[trs], inc="ve")
        h1ap = h1s[par]
        th1 = OP("scalar", lambda e, h1ap=h1ap, xpap=xpap, col=col: e.activation(
            h1ap, xpap, AF.Identity, scale=STATS[:, col + 6:col + 7],
            bias=STATS[:, col + 7:col + 8]),
            waits=[tnm, tk["h1tr"][cc - 2] if cc >= 2 else None], inc="ae")
        tk["xfree"][cc] = [th1, ts1]
        wpsT = tk["psT_free"][cc - 1] if cc >= 1 else None
        tT = None
        for j in range(8):
            tT = OP("tensor", lambda e, j=j, h1ap=h1ap: e.transpose(
                ap(psT)[:, j * 128:(j + 1) * 128], h1ap[:, j * 128:(j + 1) * 128], ap(ident)),
                waits=[th1, wpsT, t_id] if j == 0 else [], inc="pe" if j == 7 else None)
        tk["h1tr"][cc] = tT
        h1Tap = h1T[par]
        tE = OP("vector", lambda e, h1Tap=h1Tap: e.tensor_copy(h1Tap, ap(psT)),
                waits=[tT], inc="ve")

        kcol = 48 * par
        prev_tr2 = [None]
        tkh = [None, None]
        if "qkv" in skip:
            for n in range(6):
                tk["qkv_ev"][cc][n] = tE
            tk["q_ev"][cc] = tE
            tk["vready"][cc] = tE
            tk["schain"][cc] = tk["schain"][cc - 1] if cc >= 1 else tk["schain_prev"]
            tk["spref"][cc] = None
            tk["sprefdma"][cc] = None
            tk["delta"][cc] = None
            tk["ktilg"][cc] = None
            tk["khfm"][cc] = tE
            tk["psT_free"][cc] = tE
            tk["ksq_rd"][cc] = None
            continue
        for n in range(6):
            bank = ps[n % 3]
            if n >= 3:
                bfree = tk["qkv_ev"][cc][n - 3]
            elif cc > 0:
                bfree = tk["qkv_ev"][cc - 1][n + 3]
            else:
                bfree = None
            tm = None
            for k in range(8):
                tm = OP("tensor", lambda e, n=n, k=k, bank=bank, h1Tap=h1Tap: e.matmul(
                    ap(bank), h1Tap[:, k * 128:(k + 1) * 128],
                    ap(wkv)[:, k * 3 * D + n * 512: k * 3 * D + (n + 1) * 512],
                    start=(k == 0), stop=(k == 7)),
                    waits=[tE, bfree, t_wkv] if k == 0 else [], inc="pe" if k == 7 else None)
            tk["mmn"][cc][n] = tm
            if n < 2:
                tk["qkv_ev"][cc][n] = OP("scalar", lambda e, n=n, bank=bank, cc=cc: e.activation(
                    ap(q_tm)[:, cc * D + n * 512: cc * D + (n + 1) * 512], ap(bank), AF.Copy),
                    waits=[tm], inc="ae")
            elif n >= 4:
                tk["qkv_ev"][cc][n] = OP("scalar", lambda e, n=n, bank=bank, cc=cc: e.activation(
                    ap(V_tm)[:, cc * D + (n - 4) * 512: cc * D + (n - 3) * 512], ap(bank),
                    AF.Copy), waits=[tm], inc="ae")
            elif "kln" in skip:
                tk["qkv_ev"][cc][n] = OP("scalar", lambda e, n=n, bank=bank, par=par: e.activation(
                    khtm[par][:, (n - 2) * 512:(n - 1) * 512], ap(bank), AF.Copy),
                    waits=[tm], inc="ae")
                tkh[n - 2] = tk["qkv_ev"][cc][n]
            else:
                off = (n - 2) * 8
                tr1 = OP("vector", lambda e, bank=bank, off=off, kcol=kcol: e.reduce_sum(
                    KST[:, kcol + off: kcol + off + 8], seg(ap(bank), 64), axis=AX.X),
                    waits=[tm], inc="ve")
                tmu2 = OP("vector", lambda e, off=off, kcol=kcol: e.tensor_scalar(
                    KST[:, kcol + off: kcol + off + 8], KST[:, kcol + off: kcol + off + 8],
                    1.0 / DK, None, ALU.mult), waits=[tr1], inc="ve")
                tc2 = OP("vector", lambda e, bank=bank, off=off, par=par, kcol=kcol: e.tensor_tensor(
                    seg(ktmp[par], 64), seg(ap(bank), 64),
                    bc(KST[:, kcol + off: kcol + off + 8], 64), ALU.subtract),
                    waits=[tmu2], inc="ve")
                tq = OP("scalar", lambda e, par=par: e.activation(
                    ap(sqscr)[:, 0:512], ktmp[par], AF.Square),
                    waits=[tc2, prev_tr2[0]], inc="ae")
                tr2 = OP("vector", lambda e, off=off, kcol=kcol: e.reduce_sum(
                    KST[:, kcol + 16 + off: kcol + 16 + off + 8], seg(ap(sqscr)[:, 0:512], 64),
                    axis=AX.X), waits=[tq], inc="ve")
                prev_tr2[0] = tr2
                tvr2 = OP("vector", lambda e, off=off, kcol=kcol: e.tensor_scalar(
                    KST[:, kcol + 16 + off: kcol + 16 + off + 8],
                    KST[:, kcol + 16 + off: kcol + 16 + off + 8], 1.0 / DK, None, ALU.mult),
                    waits=[tr2], inc="ve")
                tsd2 = OP("scalar", lambda e, off=off, kcol=kcol: e.activation(
                    KST[:, kcol + 16 + off: kcol + 16 + off + 8],
                    KST[:, kcol + 16 + off: kcol + 16 + off + 8], AF.Sqrt, bias=ap(epsc)),
                    waits=[tvr2, t_ep], inc="ae")
                trc = OP("vector", lambda e, off=off, kcol=kcol: e.reciprocal(
                    KST[:, kcol + 16 + off: kcol + 16 + off + 8],
                    KST[:, kcol + 16 + off: kcol + 16 + off + 8]), waits=[tsd2], inc="ve")
                tkh[n - 2] = OP("vector", lambda e, off=off, n=n, par=par, kcol=kcol: e.tensor_tensor(
                    seg(khtm[par][:, (n - 2) * 512:(n - 1) * 512], 64), seg(ktmp[par], 64),
                    bc(KST[:, kcol + 16 + off: kcol + 16 + off + 8], 64), ALU.mult),
                    waits=[trc, tk["ktilg"][cc - 2] if cc >= 2 else None], inc="ve")
                tk["qkv_ev"][cc][n] = tkh[n - 2]
        tk["q_ev"][cc] = tk["qkv_ev"][cc][1]
        tk["vready"][cc] = tk["qkv_ev"][cc][5]
        tkt = OP("vector", lambda e, par=par: e.tensor_tensor(
            seg(ktil[par], 64), seg(khtm[par], 64), bc(ap(gkv16), 64), ALU.mult),
            waits=[tkh[0], tkh[1], t_gkv, tk["delta"][cc - 2] if cc >= 2 else None], inc="ve")
        tk["ktilg"][cc] = tkt
        tT2 = None
        for j in range(8):
            tT2 = OP("tensor", lambda e, j=j, par=par: e.transpose(
                ap(psT)[:, j * 128:(j + 1) * 128], khtm[par][:, j * 128:(j + 1) * 128],
                ap(ident)), waits=[tkh[0], tkh[1], tE] if j == 0 else [],
                inc="pe" if j == 7 else None)
        tE2 = None
        for j in range(8):
            tE2 = OP("vector", lambda e, j=j, cc=cc: e.tensor_copy(
                ap(khat_fm)[:, j * T + cc * 128: j * T + (cc + 1) * 128],
                ap(psT)[:, j * 128:(j + 1) * 128]),
                waits=[tT2] if j == 0 else [], inc="ve" if j == 7 else None)
        tk["khfm"][cc] = tE2
        tk["psT_free"][cc] = tE2

        if "state" in skip:
            tk["schain"][cc] = tk["schain"][cc - 1] if cc >= 1 else tk["schain_prev"]
            tk["spref"][cc] = None
            tk["sprefdma"][cc] = None
            tk["delta"][cc] = None
            tk["ktilg"][cc] = tk["ktil_skip_prev"] = None
            tk["ksq_rd"][cc] = prev_tr2[0]
            continue
        wprev = tk["schain"][cc - 1] if cc >= 1 else tk["schain_prev"]
        tsp = OP("vector", lambda e, par=par: e.tensor_copy(sprefl[par], ap(S32)),
                 waits=[wprev, tk["sprefdma"][cc - 2] if cc >= 2 else None], inc="ve")
        tspd = OP("sync", lambda e, cc=cc, par=par: e.dma_start(
            out=sprefd[cc * 128:(cc + 1) * 128, :], in_=sprefl[par]),
            waits=[tsp], inc="so", inc_n=16)
        tk["spref"][cc] = tspd
        tk["sprefdma"][cc] = tspd
        tdm = None
        for h in range(H):
            hp, s = h // 2, h % 2
            tdm = OP("tensor", lambda e, h=h, hp=hp, s=s, par=par, cc=cc: e.matmul(
                ap(ps[3])[64 * s:64 * s + 64, hp * 64:(hp + 1) * 64],
                ktil[par][:, h * 64:(h + 1) * 64],
                ap(V_tm)[:, cc * D + h * 64: cc * D + (h + 1) * 64],
                start=True, stop=True, tile_position=(0, 64 * s)),
                waits=[tkt, tk["vready"][cc], wprev] if h == 0 else [],
                inc="pe" if h == H - 1 else None)
        tk["delta"][cc] = tdm
        tu1 = OP("vector", lambda e: e.tensor_tensor(seg(ap(S32), 64), seg(ap(S32), 64),
                                                     bc(ap(gcu), 64), ALU.mult),
                 waits=[tsp, t_gcu], inc="ve")
        tu2 = OP("vector", lambda e: e.tensor_add(ap(S32), ap(S32), ap(ps[3])),
                 waits=[tu1, tdm], inc="ve")
        tk["schain"][cc] = tu2
        tk["ksq_rd"][cc] = prev_tr2[0]

    ctx.barrier()
    if stop_after == "p1":
        return _finish_stub()

    if debug:
        dbg_k = nc.dram_tensor("dbg_k", [128, 8 * T], BF16, kind="ExternalOutput" if dbg_ext else "Internal").ap()
        dbg_v = nc.dram_tensor("dbg_v", [128, NCH * D], BF16, kind="ExternalOutput" if dbg_ext else "Internal").ap()
        dbg_q = nc.dram_tensor("dbg_q", [128, NCH * D], BF16, kind="ExternalOutput" if dbg_ext else "Internal").ap()
        dbg_t = nc.dram_tensor("dbg_t", [128, 256], F32, kind="ExternalOutput" if dbg_ext else "Internal").ap()
        OP("sync", lambda e: e.dma_start(out=dbg_k, in_=ap(khat_fm)), inc="so", inc_n=16)
        OP("sync", lambda e: e.dma_start(out=dbg_v, in_=ap(V_tm)), inc="so", inc_n=16)
        OP("sync", lambda e: e.dma_start(out=dbg_q, in_=ap(q_tm)), inc="so", inc_n=16)
        OP("sync", lambda e: e.dma_start(out=dbg_t[:, 0:16], in_=ap(gkv16)), inc="so", inc_n=16)
        OP("sync", lambda e: e.dma_start(out=dbg_t[:, 16:32], in_=ap(gq16)), inc="so", inc_n=16)
        OP("sync", lambda e: e.dma_start(out=dbg_t[:, 32:40], in_=ap(gcu)), inc="so", inc_n=16)
        OP("sync", lambda e: e.dma_start(out=dbg_t[:, 40:56], in_=ap(lgcol)), inc="so", inc_n=16)
        OP("sync", lambda e: e.dma_start(out=dbg_t[:, 56:152], in_=ap(kst)), inc="so", inc_n=16)
        OP("sync", lambda e: e.dma_start(out=dbg_t[:, 152:216], in_=ap(gbm)), inc="so", inc_n=16)
        ctx.barrier()

    t_ccs = OP("sync", lambda e: e.dma_start(out=ccin[0:128, :], in_=ap(S32)),
               waits=[tk["schain"][n_chunks - 1]], inc="so", inc_n=16)
    t_ccw = OP("sync", lambda e: e.dma_start(out=ccin[128:129, 0:3], in_=ap(stats)[0:1, 28:31]),
               waits=[t_wsv], inc="so", inc_n=16)
    t_ccq = OP("vector", lambda e: e.memset(ap(sqscr)[0:1, 0:512], 0.0), inc="ve")
    t_ccz = OP("sync", lambda e: e.dma_start(out=ccin[128:129, 3:512], in_=ap(sqscr)[0:1, 0:509]),
               waits=[t_ccq], inc="so", inc_n=16)
    t_cc = OP("gpsimd", lambda e: e.collective_compute(
        "AllGather", ALU.bypass, replica_groups=[list(range(NCORES))],
        ins=[ccin.opt()], outs=[ccout.opt()]), waits=[t_ccs, t_ccw, t_ccz], inc="ge")
    t_wo = None
    for k in range(8):
        t_wo = OP("gpsimd", lambda e, k=k: e.dma_start(
            out=ap(woTs)[:, k * D:(k + 1) * D], in_=woTd[k * 128:(k + 1) * 128, :]),
            waits=[t_cc] if k == 0 else [], inc="sg", inc_n=16)
    t_sg = [None] * 8
    for j in range(8):
        t_sg[j] = OP("sync", lambda e, j=j: e.dma_start(
            out=ap(sg8)[:, j * 512:(j + 1) * 512], in_=ccout[129 * j:129 * j + 128, :]),
            waits=[t_cc], inc="sd", inc_n=16)
    t_si = OP("vector", lambda e: e.memset(ap(SINI), 0.0), inc="ve")
    for j in range(8):
        tmj = OP("vector", lambda e, j=j: e.tensor_tensor(
            seg(ap(sg8)[:, j * 512:(j + 1) * 512], 64), seg(ap(sg8)[:, j * 512:(j + 1) * 512], 64),
            bc(ap(gbm)[:, j * 8:(j + 1) * 8], 64), ALU.mult), waits=[t_sg[7], t_gb], inc="ve")
        t_si = OP("vector", lambda e, j=j: e.tensor_add(
            ap(SINI), ap(SINI), ap(sg8)[:, j * 512:(j + 1) * 512]), waits=[tmj, t_si], inc="ve")

    if debug:
        dbg_sg = nc.dram_tensor("dbg_sg", [128, 8 * 512 + 512], F32,
                                kind="ExternalOutput" if dbg_ext else "Internal").ap()
        dbg_cc = nc.dram_tensor("dbg_cc", [NCORES * 129, 512], F32,
                                kind="ExternalOutput" if dbg_ext else "Internal").ap()
        OP("sync", lambda e: e.dma_start(out=dbg_sg[:, 0:4096], in_=ap(sg8)),
           waits=[t_si], inc="so", inc_n=16)
        OP("sync", lambda e: e.dma_start(out=dbg_sg[:, 4096:4608], in_=ap(SINI)),
           waits=[t_si], inc="so", inc_n=16)
        OP("sync", lambda e: e.dma_start(out=dbg_cc, in_=ccout),
           waits=[t_cc], inc="so", inc_n=16)

    ctx.barrier()
    if stop_after == "p2":
        return _finish_stub()

    tk["yev"] = [[None] * 8 for _ in range(n_chunks)]
    tk["mkhist"] = [None] * n_chunks
    tk["sfull"] = [None] * n_chunks
    tk["uchain"] = [None] * n_chunks
    tk["p3T_free"] = [None] * n_chunks
    tk["p3mm"] = [None] * n_chunks
    tk["qts_rd"] = [None] * n_chunks
    tk["uchain_prev"] = t_si

    t_z1 = OP("vector", lambda e: e.memset(ap(qpad2), 0.0), inc="ve")
    t_z2 = OP("vector", lambda e: e.memset(ap(qtpad2), 0.0), inc="ve")
    t_z3 = OP("vector", lambda e: e.memset(sfull[0], 0.0), inc="ve")
    t_z4 = OP("vector", lambda e: e.memset(sfull[1], 0.0), inc="ve")
    tk["p3zero"] = [t_z1, t_z2, t_z3, t_z4]

    def bd3(apx, rows, blk):
        return apx[rows[0]:rows[1], :].rearrange("p (a b) -> p a b", b=128)[:, :, 0:blk] \
            if blk != 128 else apx[rows[0]:rows[1], :].rearrange("p (a b) -> p a b", b=128)

    SINIg = S32
    gcu2 = [gcu, gbl]
    texp_hist = [None, None]
    tuc_hist = [tk["schain"][n_chunks - 1], t_gb]
    def emit_sfull(cc, par, tld, pmw):
        texp = OP("scalar", lambda e, cc=cc, par=par: e.activation(
            ap(gcu2[par]), ap(lgstk), AF.Exp, scale=128.0 * cc),
            waits=[t5, tuc_hist[par]], inc="ae")
        texp_hist[par] = texp
        tuc = OP("vector", lambda e, par=par: e.tensor_tensor(
            seg(ap(SINIg), 64), seg(ap(SINI), 64), bc(ap(gcu2[par]), 64), ALU.mult),
            waits=[t_si, texp, t_ccs], inc="ve")
        tuc_hist[par] = tuc
        tk["uchain"][cc] = tuc
        tsfa = OP("vector", lambda e, par=par: e.scalar_tensor_tensor(
            bd3(sfull[par], (0, 64), 64), seg(ap(SINIg)[0:64, :], 64), 1.0,
            seg(sprefl[par][0:64, :], 64), ALU.mult, ALU.add),
            waits=[tuc, tld, tk["p3zero"], pmw], inc="ve")
        tsf = OP("vector", lambda e, par=par: e.scalar_tensor_tensor(
            sfull[par][64:128, :].rearrange("p (a b) -> p a b", b=128)[:, :, 64:128],
            seg(ap(SINIg)[64:128, :], 64), 1.0,
            seg(sprefl[par][64:128, :], 64), ALU.mult, ALU.add),
            waits=[tsfa], inc="ve")
        tk["sfull"][cc] = tsf
        return tsf

    defer_st = {}
    for cc in range(n_chunks):
        par = cc % 2
        wfree = tk["sfull"][cc - 2] if cc >= 2 else None
        tld = OP("sync", lambda e, cc=cc, par=par: e.dma_start(
            out=sprefl[par], in_=sprefd[cc * 128:(cc + 1) * 128, :]),
            waits=[tk["spref"][cc], wfree], inc="sd", inc_n=16)
        if cc >= 2:
            tsf = emit_sfull(cc, par, tld, tk["p3mm"][cc - 2])
        else:
            defer_st[cc] = tld
            tsf = None
        tqt = OP("vector", lambda e, par=par, cc=cc: e.tensor_tensor(
            seg(qts[par], 64), seg(ap(q_tm)[:, cc * D:(cc + 1) * D], 64),
            bc(ap(gq16), 64), ALU.mult),
            waits=[tk["q_ev"][cc], t_gq, tk["qts_rd"][cc - 2] if cc >= 2 else None], inc="ve")
        wT = tk["p3T_free"][cc - 1] if cc >= 1 else None
        tTq = None
        for j in range(8):
            tTq = OP("tensor", lambda e, j=j, cc=cc: e.transpose(
                ap(psT)[:, j * 128:(j + 1) * 128],
                ap(q_tm)[:, cc * D + j * 128: cc * D + (j + 1) * 128], ap(ident)),
                waits=[wT, tk["q_ev"][cc]] if j == 0 else [], inc="pe" if j == 7 else None)
        tEq = None
        for a in range(8):
            tEq = OP("scalar", lambda e, a=a: e.activation(
                ap(qpad2)[0:64, a * 256: a * 256 + 128],
                ap(psT)[0:64, a * 128:(a + 1) * 128], AF.Copy),
                waits=[tTq, tk["p3zero"]] if a == 0 else [], inc="ae" if a == 7 else None)
        for a in range(8):
            tEq = OP("scalar", lambda e, a=a: e.activation(
                ap(qpad2)[64:128, a * 256 + 128: a * 256 + 256],
                ap(psT)[64:128, a * 128:(a + 1) * 128], AF.Copy),
                waits=[], inc="ae" if a == 7 else None)
        def emit_qtpad(par2, wT2, tqt2):
            tTq2 = None
            for j in range(8):
                tTq2 = OP("tensor", lambda e, j=j, par2=par2: e.transpose(
                    ap(psT)[:, j * 128:(j + 1) * 128], qts[par2][:, j * 128:(j + 1) * 128],
                    ap(ident)), waits=[tqt2, wT2] if j == 0 else [],
                    inc="pe" if j == 7 else None)
            tEq2 = None
            for a in range(8):
                tEq2 = OP("scalar", lambda e, a=a: e.activation(
                    ap(qtpad2)[0:64, a * 256: a * 256 + 128],
                    ap(psT)[0:64, a * 128:(a + 1) * 128], AF.Copy),
                    waits=[tTq2] if a == 0 else [], inc="ae" if a == 7 else None)
            for a in range(8):
                tEq2 = OP("scalar", lambda e, a=a: e.activation(
                    ap(qtpad2)[64:128, a * 256 + 128: a * 256 + 256],
                    ap(psT)[64:128, a * 128:(a + 1) * 128], AF.Copy),
                    waits=[], inc="ae" if a == 7 else None)
            return tTq2, tEq2

        if cc >= 2:
            tTq2, tEq2 = emit_qtpad(par, tEq, tqt)
            tk["qts_rd"][cc] = tTq2
            tk["p3T_free"][cc] = tEq2
        else:
            defer_st[(cc, "tqt")] = tqt
            defer_st[(cc, "tEq")] = tEq
            tEq2 = None
            tk["p3T_free"][cc] = tEq

        tmk_l = [None] * 8
        tsc_l = {}

        def emit_tsc(hp):
            sbank = ps[hp % 2]
            if hp >= 2:
                mkfree = tmk_l[hp - 2]
            elif cc >= 1:
                mkfree = tk["mkhist"][cc - 1][hp + 6]
            else:
                mkfree = None
            tsc_l[hp] = OP("tensor", lambda e, hp=hp, sbank=sbank, cc=cc: e.matmul(
                ap(sbank)[:, 0:256],
                ap(khat_fm)[:, hp * T + cc * 128: hp * T + (cc + 1) * 128],
                ap(qpad2)[:, hp * 256:(hp + 1) * 256],
                start=True, stop=True),
                waits=[tEq, tk["khfm"][cc], mkfree], inc="pe")

        emit_tsc(0)
        for hp in range(8):
            if hp + 1 < 8:
                emit_tsc(hp + 1)
            ybank = ps[2 + hp % 2]
            apar = hp % 2
            if hp >= 2:
                sfree = tk["yev"][cc][hp - 2]
            elif cc >= 1:
                sfree = tk["yev"][cc - 1][6 + hp]
            else:
                sfree = None
            tmk = OP("vector", lambda e, hp=hp, apar=apar: e.tensor_mul(
                asb[apar], ap(ps[hp % 2])[:, 0:256], ap(dmask)[:, hp * 256:(hp + 1) * 256]),
                waits=[tsc_l[hp], t_dm, sfree], inc="ve")
            tmk_l[hp] = tmk
            if cc >= 2:
                tym = OP("tensor", lambda e, hp=hp, ybank=ybank, cc=cc, apar=apar: e.matmul(
                    ap(ybank)[:, 0:256],
                    ap(V_tm)[:, cc * D + hp * 128: cc * D + (hp + 1) * 128],
                    asb[apar], start=True, stop=False),
                    waits=[tmk, tsf, tEq2, tk["vready"][cc], sfree], inc=None)
                tym = OP("tensor", lambda e, hp=hp, ybank=ybank, par=par: e.matmul(
                    ap(ybank)[:, 0:256], sfull[par][:, hp * 128:(hp + 1) * 128],
                    ap(qtpad2)[:, hp * 256:(hp + 1) * 256],
                    start=False, stop=True), waits=[], inc="pe")
                tk["p3mm"][cc] = tym
            else:
                tym = OP("tensor", lambda e, hp=hp, ybank=ybank, cc=cc, apar=apar: e.matmul(
                    ap(ybank)[:, 0:256],
                    ap(V_tm)[:, cc * D + hp * 128: cc * D + (hp + 1) * 128],
                    asb[apar], start=True, stop=True),
                    waits=[tmk, tk["vready"][cc], sfree], inc="pe")
            tya = OP("scalar", lambda e, hp=hp, ybank=ybank, cc=cc: e.activation(
                ap(yfm)[0:64, hp * T + cc * 128: hp * T + (cc + 1) * 128],
                ap(ybank)[0:64, 0:128], AF.Copy), waits=[tym], inc="ae")
            tk["yev"][cc][hp] = OP("scalar", lambda e, hp=hp, ybank=ybank, cc=cc: e.activation(
                ap(yfm)[64:128, hp * T + cc * 128: hp * T + (cc + 1) * 128],
                ap(ybank)[64:128, 128:256], AF.Copy), waits=[tya], inc="ae")
            if hp == 3:
                wq_emit(1)
        tk["mkhist"][cc] = tmk_l
        wq_emit(1)

        if cc == 1:
            tad_hist = {}
            for dcc in (0, 1):
                dpar = dcc % 2
                tsf_d = emit_sfull(dcc, dpar, defer_st[dcc], None)
                tTq2, tEq2d = emit_qtpad(dpar, tk["p3T_free"][1],
                                         defer_st[(dcc, "tqt")])
                tk["qts_rd"][dcc] = tTq2
                tk["p3T_free"][dcc] = tEq2d
                tcx_l = [None] * 8
                for hp in range(8):
                    cbank = ps[4 + hp % 2]
                    tcx = OP("tensor", lambda e, hp=hp, cbank=cbank, dpar=dpar: e.matmul(
                        ap(cbank)[:, 0:256], sfull[dpar][:, hp * 128:(hp + 1) * 128],
                        ap(qtpad2)[:, hp * 256:(hp + 1) * 256],
                        start=True, stop=True),
                        waits=[tsf_d, tEq2d, tad_hist.get(hp % 2)], inc="pe")
                    tcx_l[hp] = tcx
                    ta1 = OP("vector", lambda e, hp=hp, cbank=cbank, dcc=dcc: e.scalar_tensor_tensor(
                        ap(yfm)[0:64, hp * T + dcc * 128: hp * T + (dcc + 1) * 128],
                        ap(cbank)[0:64, 0:128], 1.0,
                        ap(yfm)[0:64, hp * T + dcc * 128: hp * T + (dcc + 1) * 128],
                        ALU.mult, ALU.add),
                        waits=[tcx, tk["yev"][dcc][hp]], inc="ve")
                    ta2 = OP("vector", lambda e, hp=hp, cbank=cbank, dcc=dcc: e.scalar_tensor_tensor(
                        ap(yfm)[64:128, hp * T + dcc * 128: hp * T + (dcc + 1) * 128],
                        ap(cbank)[64:128, 128:256], 1.0,
                        ap(yfm)[64:128, hp * T + dcc * 128: hp * T + (dcc + 1) * 128],
                        ALU.mult, ALU.add),
                        waits=[ta1], inc="ve")
                    tad_hist[hp % 2] = ta2
                    tk["yev"][dcc][hp] = ta2
                tk["p3mm"][dcc] = tcx_l[7]
    wq_emit(32 - wq_state["i"])

    ctx.barrier()
    if stop_after == "p3":
        return _finish_stub()

    if debug:
        dbg_y = nc.dram_tensor("dbg_y", [128, 8 * T], BF16, kind="ExternalOutput" if dbg_ext else "Internal").ap()
        OP("sync", lambda e: e.dma_start(out=dbg_y, in_=ap(yfm)), inc="so", inc_n=16)
        ctx.barrier()

    tk["r2free"] = [None] * n_chunks
    tk["trn45"] = [None] * n_chunks

    def emit_n1T(cc2):
        trn2, par2 = tk["trn45"][cc2]
        wT = tk["p45T_free"][cc2 - 1] if cc2 >= 1 else None
        tT = None
        for j in range(8):
            tT = OP("tensor", lambda e, j=j, par2=par2: e.transpose(
                ap(psT)[:, j * 128:(j + 1) * 128], n1tok[par2][:, j * 128:(j + 1) * 128],
                ap(ident)), waits=[trn2, wT] if j == 0 else [], inc="pe" if j == 7 else None)
        tk["n1tr"][cc2] = tT
        tEv = None
        for j in range(8):
            tEv = OP("vector", lambda e, j=j, cc2=cc2: e.tensor_scalar(
                ap(n1T)[:, j * T + cc2 * 128: j * T + (cc2 + 1) * 128],
                ap(psT)[:, j * 128:(j + 1) * 128], -128.0, 127.0, ALU.max, ALU.min),
                waits=[tT] if j == 0 else [], inc="ve" if j == 7 else None)
        tk["n1T"][cc2] = tEv
        tk["p45T_free"][cc2] = tEv
    tk["n1T"] = [None] * n_chunks
    tk["p45T_free"] = [None] * n_chunks
    tk["r2cfree"] = [None] * n_chunks
    tk["c2free"] = [None] * n_chunks
    tk["n1tr"] = [None] * n_chunks
    tk["x2"] = [None] * n_chunks
    ST2 = ap(st2)

    for cc in range(n_chunks):
        par = cc % 2
        xpap = xs[par]
        xfree = tk["r2free"][cc - 2] if cc >= 2 else None
        tx2 = OP("sync", lambda e, cc=cc, xpap=xpap: e.dma_start(
            out=xpap, in_=xd[cc * 128:(cc + 1) * 128, :]), waits=[xfree], inc="sd", inc_n=16)
        tk["x2"][cc] = tx2
        col = 16 * par
        evs = []
        for half in range(2):
            bank = ps[4] if half == 0 else ps[5]
            bfree = tk["r2free"][cc - 1] if cc >= 1 else None
            tm = None
            for k in range(8):
                tm = OP("tensor", lambda e, half=half, k=k, bank=bank, cc=cc: e.matmul(
                    ap(bank), ap(yfm)[:, k * T + cc * 128: k * T + (cc + 1) * 128],
                    ap(woTs)[:, k * D + half * 512: k * D + (half + 1) * 512],
                    start=(k == 0), stop=(k == 7)),
                    waits=[tk["yev"][cc][k], bfree, t_wo] if k == 0 else [],
                    inc="pe" if k == 7 else None)
            tev = OP("vector", lambda e, half=half, bank=bank, col=col, par=par: e.scalar_tensor_tensor(
                r2c[par][:, half * 512:(half + 1) * 512], ap(bank), 1.0,
                xs[par][:, half * 512:(half + 1) * 512], ALU.mult, ALU.add,
                accum_out=ST2[:, col + half: col + half + 1]),
                waits=[tm, tx2, tk["r2cfree"][cc - 2] if cc >= 2 else None], inc="ve")
            evs.append(tev)
        r2ap = r2c[par]
        tspill = OP("sync", lambda e, cc=cc, r2ap=r2ap: e.dma_start(
            out=r2d[cc * 128:(cc + 1) * 128, :], in_=r2ap), waits=evs, inc="so", inc_n=16)
        tqq = OP("scalar", lambda e, r2ap=r2ap, col=col: e.activation(
            ap(sqscr), r2ap, AF.Square, accum_out=ST2[:, col + 2:col + 3]),
            waits=evs, inc="ae")
        tad = OP("vector", lambda e, col=col: e.tensor_add(
            ST2[:, col + 3:col + 4], ST2[:, col:col + 1], ST2[:, col + 1:col + 2]),
            waits=evs, inc="ve")
        tmu = OP("vector", lambda e, col=col: e.tensor_scalar(
            ST2[:, col + 3:col + 4], ST2[:, col + 3:col + 4], 1.0 / D, None, ALU.mult),
            waits=[tad], inc="ve")
        tms = OP("vector", lambda e, col=col: e.tensor_mul(
            ST2[:, col + 4:col + 5], ST2[:, col + 3:col + 4], ST2[:, col + 3:col + 4]),
            waits=[tmu], inc="ve")
        tvr = OP("vector", lambda e, col=col: e.scalar_tensor_tensor(
            ST2[:, col + 5:col + 6], ST2[:, col + 2:col + 3], 1.0 / D,
            ST2[:, col + 4:col + 5], ALU.mult, ALU.subtract), waits=[tms, tqq], inc="ve")
        tsd = OP("scalar", lambda e, col=col: e.activation(
            ST2[:, col + 5:col + 6], ST2[:, col + 5:col + 6], AF.Sqrt, bias=ap(epsc)),
            waits=[tvr, t_ep], inc="ae")
        trs = OP("vector", lambda e, col=col: e.reciprocal(
            ST2[:, col + 6:col + 7], ST2[:, col + 5:col + 6]), waits=[tsd], inc="ve")
        tc2 = OP("vector", lambda e, par=par, r2ap=r2ap, col=col: e.tensor_scalar_sub(
            c2s[par], r2ap, ST2[:, col + 3:col + 4]),
            waits=[tmu, tspill, tk["c2free"][cc - 2] if cc >= 2 else None], inc="ve")
        tam = OP("vector", lambda e, par=par, col=col: e.reduce_max(
            ST2[:, col + 7:col + 8], c2s[par], axis=AX.X, apply_absolute_value=True),
            waits=[tc2], inc="ve")
        tk["r2free"][cc] = [tc2, tqq]
        tk["r2cfree"][cc] = [tqq, tspill]
        tm1 = OP("vector", lambda e, col=col: e.tensor_tensor(
            ST2[:, col + 8:col + 9], ST2[:, col + 6:col + 7], ST2[:, col + 7:col + 8], ALU.mult),
            waits=[trs, tam], inc="ve")
        tm2 = OP("vector", lambda e, col=col: e.tensor_scalar(
            ST2[:, col + 8:col + 9], ST2[:, col + 8:col + 9], 1e-5, None, ALU.max),
            waits=[tm1], inc="ve")
        tdq = OP("vector", lambda e, cc=cc, col=col: e.tensor_scalar(
            ap(deq1c)[:, cc:cc + 1], ST2[:, col + 8:col + 9], 1.0 / 127.0, None, ALU.mult),
            waits=[tm2], inc="ve")
        tmi = OP("vector", lambda e, col=col: e.reciprocal(
            ST2[:, col + 9:col + 10], ST2[:, col + 8:col + 9]), waits=[tm2], inc="ve")
        tu2 = OP("vector", lambda e, col=col: e.scalar_tensor_tensor(
            ST2[:, col + 10:col + 11], ST2[:, col + 6:col + 7], 127.0,
            ST2[:, col + 9:col + 10], ALU.mult, ALU.mult), waits=[tmi], inc="ve")
        tm1a = OP("scalar", lambda e, par=par, col=col: e.activation(
            c2s[par], c2s[par], AF.Identity, scale=ST2[:, col + 10:col + 11],
            bias=ap(mcol)), waits=[tu2, tam, t_mc], inc="ae")
        trn = OP("vector", lambda e, par=par: e.tensor_scalar(
            n1tok[par], c2s[par], MAGIC, None, ALU.subtract),
            waits=[tm1a, tk["n1tr"][cc - 2] if cc >= 2 else None], inc="ve")
        tk["c2free"][cc] = trn
        tk["trn45"][cc] = (trn, par)
        if cc >= 1:
            emit_n1T(cc - 1)
        if cc == n_chunks - 1:
            emit_n1T(cc)

    if debug:
        dbg_n1 = nc.dram_tensor("dbg_n1", [128, 8 * T], BF16, kind="ExternalOutput" if dbg_ext else "Internal").ap()
        dbg_s2 = nc.dram_tensor("dbg_s2", [128, 64], F32, kind="ExternalOutput" if dbg_ext else "Internal").ap()
        OP("sync", lambda e: e.dma_start(out=dbg_n1, in_=ap(n1T)),
           waits=[tk["n1T"][n_chunks - 1]], inc="so", inc_n=16)
        OP("sync", lambda e: e.dma_start(out=dbg_s2[:, 0:16], in_=ap(deq1c)),
           waits=[tk["n1T"][n_chunks - 1]], inc="so", inc_n=16)
        OP("sync", lambda e: e.dma_start(out=dbg_s2[:, 16:20], in_=ap(mqcol)),
           waits=[tk["sw"]], inc="so", inc_n=16)
        OP("sync", lambda e: e.dma_start(out=dbg_s2[:, 20:23], in_=ap(swcol)[:, 0:3]),
           waits=[tk["sw"]], inc="so", inc_n=16)
        ctx.barrier()

    t_ga = OP("vector", lambda e: e.tensor_scalar_mul(ap(gscA), ap(deq1c), ap(mqcol)[:, 0:1]),
              waits=[tk["sw"], tk["n1T"][n_chunks - 1]], inc="ve")
    t_ua = OP("vector", lambda e: e.tensor_scalar_mul(ap(uscA), ap(deq1c), ap(mqcol)[:, 1:2]),
              waits=[t_ga], inc="ve")
    tk["ffnscales"] = t_ua

    ctx.barrier()
    if stop_after == "p45":
        return _finish_stub()

    AMX = ap(amx2)
    prev_out = [[None] * KSUP for _ in range(NSUP)]
    t_final_outs = []
    pdone = {}
    last_pstt = None
    psT_prev = [None]
    n2_prev = {}
    dbank_prev = {}
    wdqg_read = {}
    prev_tmm = [None]
    gulast = {"g": None, "u": None}

    for sup in range(NSUP):
        ccs = list(range(sup * KSUP, (sup + 1) * KSUP))
        pamax = {}
        for g in range(4):
            if sup == 0:
                wq_emit(4)
            tqg = []
            tqu = []
            for k in range(8):
                eng, sem = ("scalar", "ad") if k % 2 == 0 else ("gpsimd", "sg")
                tqg.append(OP(eng, lambda e, g=g, k=k: e.dma_start(
                    out=ap(wgq)[:, k * 1024:(k + 1) * 1024],
                    in_=wqgd[k * 128:(k + 1) * 128, g * 1024:(g + 1) * 1024]),
                    waits=[gulast["g"], wq_state["tick"][(0, k * 128, g * 1024)]],
                    inc=sem, inc_n=16))
            for k in range(8):
                eng, sem = ("scalar", "ad") if k % 2 == 1 else ("gpsimd", "sg")
                tqu.append(OP(eng, lambda e, g=g, k=k: e.dma_start(
                    out=ap(wuq)[:, k * 1024:(k + 1) * 1024],
                    in_=wqud[k * 128:(k + 1) * 128, g * 1024:(g + 1) * 1024]),
                    waits=[gulast["u"], wq_state["tick"][(1, k * 128, g * 1024)]],
                    inc=sem, inc_n=16))
            for ti, cc in enumerate(ccs):
                for half in range(2):
                    gbank = ps[half]
                    ubank = ps[2 + half]
                    hpar = half
                    tgm = None
                    for k in range(8):
                        tgm = OP("tensor", lambda e, k=k, half=half, cc=cc, gbank=gbank: e.matmul(
                            ap(gbank), ap(n1T)[:, k * T + cc * 128: k * T + (cc + 1) * 128],
                            ap(wgq)[:, k * 1024 + half * 512: k * 1024 + (half + 1) * 512],
                            start=(k == 0), stop=(k == 7)),
                            waits=[tqg[k], tk["n1T"][cc], pdone.get(("gb", half))]
                            if k == 0 else [tqg[k]],
                            inc="pe" if k == 7 else None)
                    tum = None
                    for k in range(8):
                        tum = OP("tensor", lambda e, k=k, half=half, cc=cc, ubank=ubank: e.matmul(
                            ap(ubank), ap(n1T)[:, k * T + cc * 128: k * T + (cc + 1) * 128],
                            ap(wuq)[:, k * 1024 + half * 512: k * 1024 + (half + 1) * 512],
                            start=(k == 0), stop=(k == 7)),
                            waits=[tqu[k], pdone.get(("ub", half))] if k == 0 else [tqu[k]],
                            inc="pe" if k == 7 else None)
                    tsl = OP("scalar", lambda e, cc=cc, gbank=gbank, hpar=hpar: e.activation(
                        gsilu[hpar], ap(gbank), AF.Silu, scale=ap(gscA)[:, cc:cc + 1]),
                        waits=[tgm, tk["ffnscales"], pdone.get(("ub", half))], inc="ae")
                    pslice_off = ti * INNER + g * 1024 + half * 512
                    tpp = OP("vector", lambda e, cc=cc, ubank=ubank, hpar=hpar, o=pslice_off: e.scalar_tensor_tensor(
                        ap(p_sb)[:, o:o + 512], ap(ubank), ap(uscA)[:, cc:cc + 1],
                        gsilu[hpar], ALU.mult, ALU.mult),
                        waits=[tum, tsl], inc="ve")
                    pdone[("gb", half)] = tsl
                    pdone[("ub", half)] = tpp
                    last_pstt = tpp
                if g == 0:
                    pamax[ti] = OP("vector", lambda e, ti=ti: e.reduce_max(
                        AMX[:, ti:ti + 1],
                        ap(p_sb)[:, ti * INNER: ti * INNER + 1024],
                        axis=AX.X, apply_absolute_value=True),
                        waits=[last_pstt], inc="ve")
                else:
                    OP("vector", lambda e, ti=ti, g=g: e.reduce_max(
                        AMX[:, 12:13],
                        ap(p_sb)[:, ti * INNER + g * 1024: ti * INNER + (g + 1) * 1024],
                        axis=AX.X, apply_absolute_value=True),
                        waits=[last_pstt], inc="ve")
                    pamax[ti] = OP("vector", lambda e, ti=ti: e.tensor_tensor(
                        AMX[:, ti:ti + 1], AMX[:, ti:ti + 1], AMX[:, 12:13],
                        ALU.max), inc="ve")
            gulast["g"] = tgm
            gulast["u"] = tum
        tquant = {}
        for ti, cc in enumerate(ccs):
            tm2 = OP("vector", lambda e, ti=ti: e.tensor_scalar(
                AMX[:, ti:ti + 1], AMX[:, ti:ti + 1], 1e-5, None, ALU.max),
                waits=[pamax[ti]], inc="ve")
            tu3 = OP("vector", lambda e, ti=ti: e.reciprocal(
                AMX[:, 4 + ti:5 + ti], AMX[:, ti:ti + 1]),
                waits=[tm2, prev_tmm[0]], inc="ve")
            tu3b = OP("vector", lambda e, ti=ti: e.tensor_scalar(
                AMX[:, 4 + ti:5 + ti], AMX[:, 4 + ti:5 + ti], 127.0, None, ALU.mult),
                waits=[tu3], inc="ve")
            tds = OP("vector", lambda e, ti=ti: e.tensor_scalar_mul(
                AMX[:, 8 + ti:9 + ti], AMX[:, ti:ti + 1], ap(mqcol)[:, 3:4]),
                waits=[tm2, tk["sw"]], inc="ve")
            tq_hist = [None, None]
            tq = None
            for qtr in range(4):
                qpar = qtr % 2
                tmm = OP("scalar", lambda e, ti=ti, qtr=qtr, qpar=qpar: e.activation(
                    c6s[qpar], ap(p_sb)[:, ti * INNER + qtr * 1024: ti * INNER + (qtr + 1) * 1024],
                    AF.Identity, scale=AMX[:, 4 + ti:5 + ti], bias=ap(mcol)),
                    waits=[tu3b, tq_hist[qpar], t_mc], inc="ae")
                tq = OP("vector", lambda e, ti=ti, qtr=qtr, qpar=qpar: e.tensor_scalar(
                    ap(p_sb)[:, ti * INNER + qtr * 1024: ti * INNER + (qtr + 1) * 1024],
                    c6s[qpar], MAGIC, None, ALU.subtract),
                    waits=[tmm], inc="ve")
                tq_hist[qpar] = tq
            tquant[ti] = (tq, tds)
        prev_tmm[0] = tmm
        tracc = {}
        for ti, cc in enumerate(ccs):
            tracc[ti] = OP("sync", lambda e, ti=ti, cc=cc: e.dma_start(
                out=ap(racc)[:, ti * 1024:(ti + 1) * 1024], in_=r2d[cc * 128:(cc + 1) * 128, :]),
                waits=[prev_out[sup - 1][ti] if sup >= 1 else None], inc="sd", inc_n=16)
        for kg in range(4):
            kpar = kg % 2
            wldds = []
            for k8 in range(8):
                eng, sem = ("scalar", "ad") if k8 % 2 == 0 else ("gpsimd", "sg")
                wl_waits = []
                if k8 <= 1:
                    wl_waits = [wdqg_read.get(kpar)]
                if sup == 0:
                    wl_waits = wl_waits + [wq_state["tick"][(2, (kg * 8 + k8) * 128, 0)]]
                wldds.append(OP(eng, lambda e, kg=kg, k8=k8, kpar=kpar: e.dma_start(
                    out=wdqg[kpar][:, k8 * 1024:(k8 + 1) * 1024],
                    in_=wqdd[(kg * 8 + k8) * 128:(kg * 8 + k8 + 1) * 128, :]),
                    waits=wl_waits, inc=sem, inc_n=16))
            tevs = []
            for ti, cc in enumerate(ccs):
                tpar = ti % 2
                tT = None
                for j in range(8):
                    tT = OP("tensor", lambda e, ti=ti, kg=kg, j=j: e.transpose(
                        ap(psT)[:, j * 128:(j + 1) * 128],
                        ap(p_sb)[:, ti * INNER + kg * 1024 + j * 128: ti * INNER + kg * 1024 + (j + 1) * 128],
                        ap(ident)), waits=[tquant[ti][0], psT_prev[0], n2_prev.get(tpar)]
                        if j == 0 else [], inc="pe" if j == 7 else None)
                tEv = None
                for j in range(8):
                    tEv = OP("vector", lambda e, j=j, tpar=tpar: e.tensor_scalar(
                        n2Ts[tpar][:, j * 128:(j + 1) * 128],
                        ap(psT)[:, j * 128:(j + 1) * 128], -128.0, 127.0, ALU.max, ALU.min),
                        waits=[tT] if j == 0 else [], inc="ve" if j == 7 else None)
                for half in range(2):
                    dbank = ps[4 + half]
                    tdm = None
                    for k8 in range(8):
                        tdm = OP("tensor", lambda e, k8=k8, half=half, tpar=tpar, kpar=kpar, dbank=dbank: e.matmul(
                            ap(dbank), n2Ts[tpar][:, k8 * 128:(k8 + 1) * 128],
                            wdqg[kpar][:, k8 * 1024 + half * 512: k8 * 1024 + (half + 1) * 512],
                            start=(k8 == 0), stop=(k8 == 7)),
                            waits=[tEv, wldds[k8], tracc[ti], dbank_prev.get(half)]
                            if k8 == 0 else [wldds[k8]], inc="pe" if k8 == 7 else None)
                    tev = OP("vector", lambda e, ti=ti, half=half, dbank=dbank: e.scalar_tensor_tensor(
                        ap(racc)[:, ti * 1024 + half * 512: ti * 1024 + (half + 1) * 512],
                        ap(dbank), AMX[:, 8 + ti:9 + ti],
                        ap(racc)[:, ti * 1024 + half * 512: ti * 1024 + (half + 1) * 512],
                        ALU.mult, ALU.add), waits=[tdm, tquant[ti][1]], inc="ve")
                    tevs.append(tev)
                    n2_prev[tpar] = tdm
                    dbank_prev[half] = tev
            wdqg_read[kpar] = n2_prev[(KSUP - 1) % 2]
            if kg == 3:
                for ti, cc in enumerate(ccs):
                    tout = OP("sync", lambda e, ti=ti, cc=cc: e.dma_start(
                        out=outd[cc * 128:(cc + 1) * 128, :],
                        in_=ap(racc)[:, ti * 1024:(ti + 1) * 1024]),
                        waits=[tevs[2 * ti], tevs[2 * ti + 1]], inc="so", inc_n=16)
                    prev_out[sup][ti] = tout
                    t_final_outs.append(tout)

    OP("sync", lambda e: e.nop(), waits=[t_final_outs[-1],
                                         ("so", ctx.cnt.get("so", 0)),
                                         ("sd", ctx.cnt.get("sd", 0)),
                                         ("sg", ctx.cnt.get("sg", 0)),
                                         ("ad", ctx.cnt.get("ad", 0))])

    build._last_ctx = ctx
    build._last_tk = tk
    with nc.Block() as block:
        ctx.emit_all(block)
    return nc


_CACHE = {}


def kernel(L0_attn_x, L0_attn_W_qkv, L0_attn_W_o, L0_attn_log_gamma,
           L0_ffn_w_gate, L0_ffn_w_up, L0_ffn_w_down):
    x = np.ascontiguousarray(np.asarray(L0_attn_x, dtype=np.float32).reshape(NCORES * T, D))
    wqkvT = np.ascontiguousarray(np.asarray(L0_attn_W_qkv, dtype=np.float32).T)
    woT = np.ascontiguousarray(np.asarray(L0_attn_W_o, dtype=np.float32).T)
    wgT = np.ascontiguousarray(np.asarray(L0_ffn_w_gate, dtype=np.float32).T)
    wuT = np.ascontiguousarray(np.asarray(L0_ffn_w_up, dtype=np.float32).T)
    wdT = np.ascontiguousarray(np.asarray(L0_ffn_w_down, dtype=np.float32).T)
    lg = np.ascontiguousarray(np.asarray(L0_attn_log_gamma, dtype=np.float32).reshape(1, H))

    if "nc" not in _CACHE:
        _CACHE["nc"] = build()
    nc = _CACHE["nc"]

    in_maps = []
    for c in range(NCORES):
        wsl = np.ascontiguousarray(np.concatenate([
            wgT[c * 128:(c + 1) * 128, :],
            wuT[c * 128:(c + 1) * 128, :],
            wdT[c * 512:(c + 1) * 512, :].reshape(128, INNER)], axis=0))
        in_maps.append({
            "x": x[c * T:(c + 1) * T],
            "wqkvT": wqkvT, "woT": woT, "wgT": wgT, "wuT": wuT, "wdT": wdT,
            "lg": lg, "wsl": wsl,
        })
    res = run_bass_kernel_spmd(nc, in_maps, core_ids=list(range(NCORES)))
    out = np.concatenate([r["out"] for r in res.results], axis=0)
    return out.reshape(1, NCORES * T, D).astype(np.float32)
